# version 09
# baseline (speedup 1.0000x reference)
import numpy as np
import ml_dtypes
from contextlib import ExitStack

import concourse.bass as bass
import concourse.mybir as mybir
from concourse.bass_utils import run_bass_kernel_spmd

F32 = mybir.dt.float32
BF16 = mybir.dt.bfloat16
I32 = mybir.dt.int32
ALU = mybir.AluOpType
AF = mybir.ActivationFunctionType
AX = mybir.AxisListType

B, S, H, II = 4, 2048, 2048, 8192
TOK = 1024
NHC = H // 128
LN2 = float(np.log(2.0))
INVLN2 = float(1.0 / np.log(2.0))
B52 = 6755399441055744.0
B23 = 12582912.0
RSQH = float(1.0 / np.sqrt(np.float64(H)))
EPS = 1e-5

PHASES = 7

ENGINES = ("sp", "pe", "dve", "act", "pl")
ENG_ATTR = {"sp": "sync", "pe": "tensor", "dve": "vector", "act": "scalar",
            "pl": "gpsimd"}


def _flat(ws):
    out = {}
    stack = list(ws)
    while stack:
        w = stack.pop()
        if w is None:
            continue
        if isinstance(w, (list, tuple)) and not (
                len(w) == 2 and isinstance(w[0], str)):
            stack.extend(w)
        else:
            eng, v = w
            if out.get(eng, -1) < v:
                out[eng] = v
    return [(e, v) for e, v in out.items()]


class Plan:
    def __init__(self, nc, es):
        self.nc = nc
        self.es = es
        self.sem = {e: es.enter_context(nc.semaphore("sem_" + e))
                    for e in ENGINES}
        self.cnt = {e: 0 for e in ENGINES}
        self.q = {e: [] for e in ENGINES}

    def _sem(self, name):
        if name not in self.sem:
            self.sem[name] = self.es.enter_context(
                self.nc.semaphore("sem_" + name))
            self.cnt[name] = 0
        return self.sem[name]

    def emit(self, eng, fn, waits=(), inc=1, allow_self=False, sem=None):
        s = eng if sem is None else sem
        self._sem(s)
        waits = _flat(waits)
        if eng in ("dve", "act") and fn is not None:
            waits = _flat([waits, (eng, self.cnt[eng])])
            allow_self = True
        self.cnt[s] += inc
        sems = self.sem

        def run(e, fn=fn, waits=tuple(waits), inc=inc, eng=eng, s=s,
                allow_self=allow_self):
            for (we, wv) in waits:
                if we != eng or allow_self:
                    e.wait_ge(sems[we], wv)
            if fn is not None:
                inst = fn(e)
                if inc:
                    inst.then_inc(sems[s], inc)

        self.q[eng].append(run)
        return (s, self.cnt[s])

    def mark(self, eng):
        self._sem(eng)
        return (eng, self.cnt[eng])

    def join(self):
        return [(k, v) for k, v in self.cnt.items()]

    def flush(self, blk):
        for eng in ENGINES:
            if self.q[eng]:
                lst = self.q[eng]
                self.q[eng] = []
                getattr(blk, ENG_ATTR[eng])(
                    lambda e, lst=lst: [r(e) for r in lst])


class PsumPool:
    def __init__(self, banks):
        self.banks = banks
        self.guard = [None] * len(banks)
        self.i = 0

    def next(self):
        i = self.i % len(self.banks)
        self.i += 1
        return i, self.banks[i], self.guard[i]

    def consumed(self, i, tok):
        self.guard[i] = tok



class Builder:

    def __init__(self, nc, P, es):
        self.nc = nc
        self.P = P
        sbg = lambda name, shape, dt: es.enter_context(
            nc.sbuf_tensor(name, shape, dt))
        self.ln_s = sbg("ln_s", [128, 8], F32)
        self.qs = [sbg(f"qs{i}", [128, 64], F32) for i in range(8)]
        self.qind = sbg("qind", [128, 4], F32)
        self._ind_done = False
        self.scope_ln = ExitStack()
        self.ln_xc = self.scope_ln.enter_context(
            nc.sbuf_tensor("ln_xc", [128, H], F32))
        self.ln_sq = self.scope_ln.enter_context(
            nc.sbuf_tensor("ln_sq", [128, H], F32))
        self.scope_q = ExitStack()
        self.qf = [self.scope_q.enter_context(
            nc.sbuf_tensor(f"qf{i}", [128, 2048], F32)) for i in range(4)]

    def emit_ln(self, x_dram, row0, dst_sb, dst_c0, head_waits=()):
        t_ld = self.P.emit("sp", lambda e: e.dma_start(
            self.ln_xc[:, 0:H], x_dram[row0:row0 + 128, :]),
            waits=head_waits, inc=16, sem="s_ln")
        self.P.emit("dve", lambda e: e.reduce_sum(
            self.ln_s[:, 0:1], self.ln_xc[:, 0:H], axis=AX.X), waits=[t_ld])
        self.P.emit("dve", lambda e: e.tensor_scalar_mul(
            self.ln_s[:, 1:2], self.ln_s[:, 0:1], -1.0 / H))
        t3 = self.P.emit("dve", lambda e: e.tensor_scalar_add(
            self.ln_xc[:, 0:H], self.ln_xc[:, 0:H], self.ln_s[:, 1:2]))
        t4 = self.P.emit("act", lambda e: e.activation(
            self.ln_sq[:, 0:H], self.ln_xc[:, 0:H], AF.Square,
            accum_out=self.ln_s[:, 2:3]), waits=[t3])
        t5 = self.P.emit("dve", lambda e: e.tensor_scalar(
            self.ln_s[:, 3:4], self.ln_s[:, 2:3], 1.0 / H, EPS, ALU.mult, ALU.add),
            waits=[t4])
        t6 = self.P.emit("act", lambda e: e.activation(
            self.ln_s[:, 4:5], self.ln_s[:, 3:4], AF.Sqrt), waits=[t5])
        self.P.emit("dve", lambda e: e.reciprocal(
            self.ln_s[:, 5:6], self.ln_s[:, 4:5]), waits=[t6])
        self.P.emit("dve", lambda e: e.tensor_mul(
            self.ln_s[:, 6:7], self.ln_s[:, 5:6], self.ln_s[:, 5:6]))
        self.P.emit("dve", lambda e: e.tensor_mul(
            self.ln_s[:, 7:8], self.ln_s[:, 6:7], self.ln_s[:, 3:4]))
        self.P.emit("dve", lambda e: e.tensor_scalar(
            self.ln_s[:, 6:7], self.ln_s[:, 7:8], -0.5, 1.5, ALU.mult, ALU.add))
        self.P.emit("dve", lambda e: e.tensor_mul(
            self.ln_s[:, 7:8], self.ln_s[:, 5:6], self.ln_s[:, 6:7]))
        tw = self.P.emit("dve", lambda e: e.tensor_scalar_mul(
            dst_sb[:, dst_c0:dst_c0 + H], self.ln_xc[:, 0:H], self.ln_s[:, 7:8]))
        return tw

    def emit_ind(self):
        if self._ind_done:
            return
        self._ind_done = True
        self.P.emit("dve", lambda e: e.memset(self.qind[:, 0:1], 1.0))
        self.P.emit("dve", lambda e: e.tensor_scalar(
            self.qind[:, 0:1], self.qind[:, 0:1], B52, -B52,
            ALU.add, ALU.add))
        self.P.emit("dve", lambda e: e.tensor_scalar(
            self.qind[:, 1:2], self.qind[:, 0:1], -1.0, 1.0,
            ALU.mult, ALU.add))
        self.P.emit("dve", lambda e: e.tensor_scalar_mul(
            self.qind[:, 2:3], self.qind[:, 1:2], B23))
        self.P.emit("dve", lambda e: e.scalar_tensor_tensor(
            self.qind[:, 2:3], self.qind[:, 0:1], B52, self.qind[:, 2:3],
            ALU.mult, ALU.add))

    def emit_round(self, t_ap, r_ap, tmp_ap, waits):
        self.emit_ind()
        self.P.emit("dve", lambda e: e.tensor_scalar(
            tmp_ap, t_ap, B23, -B23, ALU.add, ALU.add), waits=waits)
        self.P.emit("dve", lambda e: e.tensor_scalar(
            r_ap, t_ap, B52, -B52, ALU.add, ALU.add))
        tr = self.P.emit("dve", lambda e: e.scalar_tensor_tensor(
            r_ap, tmp_ap, self.qind[:, 1:2], r_ap, ALU.mult, ALU.add))
        return tr

    def emit_quant(self, src_sb, src_c0, dst_sb, dst_c0, F, m, waits,
                   dst_guard=None):
        nb = F // 32
        cmax = (7.5 if m == 3 else 6.0) * (2.0 ** m)
        seg = lambda t, c0: t[:, c0:c0 + F].rearrange(
            "p (b c) -> p b c", c=32)
        bc = lambda t: t[:, 0:nb].unsqueeze(2).broadcast_to([128, nb, 32])
        self.emit_ind()
        P, qs, qf = self.P, self.qs, self.qf
        bm = self.qind[:, 2:3]
        P.emit("dve", lambda e: e.reduce_max(
            qs[0][:, 0:nb].unsqueeze(2), seg(src_sb, src_c0),
            axis=AX.X, apply_absolute_value=True), waits=waits)
        P.emit("dve", lambda e: e.tensor_single_scalar(
            qs[1][:, 0:nb], qs[0][:, 0:nb], 0.0, ALU.is_equal))
        t1 = P.emit("dve", lambda e: e.tensor_tensor(
            qs[2][:, 0:nb], qs[0][:, 0:nb], qs[1][:, 0:nb], ALU.add))
        t2 = P.emit("act", lambda e: e.activation(
            qs[3][:, 0:nb], qs[2][:, 0:nb], AF.Ln), waits=[t1])
        t3 = P.emit("dve", lambda e: e.tensor_scalar(
            qs[4][:, 0:nb], qs[3][:, 0:nb], INVLN2, 1e-9 - 0.5,
            ALU.mult, ALU.add), waits=[t2])
        te = self.emit_round(qs[4][:, 0:nb], qs[1][:, 0:nb],
                             qs[5][:, 0:nb], waits=[t3])
        P.emit("dve", lambda e: e.tensor_scalar(
            qs[4][:, 0:nb], qs[1][:, 0:nb], float(m + 2 - 31),
            float(m + 2 + 31), ALU.max, ALU.min), waits=[te])
        P.emit("dve", lambda e: e.tensor_single_scalar(
            qs[5][:, 0:nb], qs[4][:, 0:nb], float(m + 2), ALU.subtract))
        t5 = P.emit("act", lambda e: e.activation(
            qs[2][:, 0:nb], qs[5][:, 0:nb], AF.Exp, scale=-LN2),
            waits=[P.mark("dve")])
        t6 = P.emit("act", lambda e: e.activation(
            qs[3][:, 0:nb], qs[5][:, 0:nb], AF.Exp, scale=LN2))
        P.emit("dve", lambda e: e.tensor_single_scalar(
            qs[6][:, 0:nb], qs[5][:, 0:nb], 0.0, ALU.is_le))
        P.emit("dve", lambda e: e.tensor_tensor(
            qs[7][:, 0:nb], qs[2][:, 0:nb], qs[3][:, 0:nb], ALU.subtract),
            waits=[t5, t6])
        P.emit("dve", lambda e: e.tensor_tensor(
            qs[7][:, 0:nb], qs[7][:, 0:nb], qs[6][:, 0:nb], ALU.mult))
        P.emit("dve", lambda e: e.tensor_tensor(
            qs[7][:, 0:nb], qs[7][:, 0:nb], qs[3][:, 0:nb], ALU.add))
        P.emit("dve", lambda e: e.tensor_single_scalar(
            qs[7][:, 0:nb], qs[7][:, 0:nb], bm, ALU.add))
        P.emit("dve", lambda e: e.tensor_single_scalar(
            qs[7][:, 0:nb], qs[7][:, 0:nb], bm, ALU.subtract))
        P.emit("dve", lambda e: e.reciprocal(
            qs[4][:, 0:nb], qs[7][:, 0:nb]))
        P.emit("dve", lambda e: e.tensor_tensor(
            qs[2][:, 0:nb], qs[7][:, 0:nb], qs[4][:, 0:nb], ALU.subtract))
        P.emit("dve", lambda e: e.tensor_tensor(
            qs[2][:, 0:nb], qs[2][:, 0:nb], qs[6][:, 0:nb], ALU.mult))
        P.emit("dve", lambda e: e.tensor_tensor(
            qs[2][:, 0:nb], qs[2][:, 0:nb], qs[4][:, 0:nb], ALU.add))
        P.emit("dve", lambda e: e.tensor_tensor(
            qs[3][:, 0:nb], qs[7][:, 0:nb], qs[4][:, 0:nb], ALU.add))
        tS = P.emit("dve", lambda e: e.tensor_tensor(
            qs[3][:, 0:nb], qs[3][:, 0:nb], qs[2][:, 0:nb], ALU.subtract))
        t7 = P.emit("dve", lambda e: e.tensor_tensor(
            seg(qf[0], 0), seg(src_sb, src_c0), bc(qs[2]), ALU.mult))
        t8 = P.emit("act", lambda e: e.activation(
            qf[1][:, 0:F], qf[0][:, 0:F], AF.Abs), waits=[t7])
        P.emit("dve", lambda e: e.tensor_single_scalar(
            qf[2][:, 0:F], qf[1][:, 0:F], 2.0 * 2 ** m, ALU.is_ge),
            waits=[t8])
        tg4 = P.emit("dve", lambda e: e.tensor_single_scalar(
            qf[3][:, 0:F], qf[1][:, 0:F], 4.0 * 2 ** m, ALU.is_ge))
        tl1 = P.emit("act", lambda e: e.activation(
            qf[3][:, 0:F], qf[3][:, 0:F], AF.Copy, scale=2.0, bias=1.0),
            waits=[tg4])
        tl2 = P.emit("dve", lambda e: e.tensor_tensor(
            qf[3][:, 0:F], qf[3][:, 0:F], qf[2][:, 0:F], ALU.add),
            waits=[tl1])
        tl3 = P.emit("act", lambda e: e.activation(
            qf[3][:, 0:F], qf[3][:, 0:F], AF.Copy, scale=bm), waits=[tl2])
        tadd = P.emit("pl", lambda e: e.tensor_tensor(
            qf[1][:, 0:F], qf[0][:, 0:F], qf[3][:, 0:F], ALU.add),
            waits=[tl3])
        tsub = P.emit("pl", lambda e: e.tensor_tensor(
            qf[1][:, 0:F], qf[1][:, 0:F], qf[3][:, 0:F], ALU.subtract),
            waits=[tadd], allow_self=True)
        P.emit("dve", lambda e: e.tensor_scalar(
            qf[1][:, 0:F], qf[1][:, 0:F], -cmax, cmax, ALU.max, ALU.min),
            waits=[tsub])
        tv = P.emit("dve", lambda e: e.tensor_tensor(
            seg(dst_sb, dst_c0), seg(qf[1], 0), bc(qs[3]), ALU.mult),
            waits=[tS, dst_guard])
        return tv

    def emit_T(self, pool, src_sb, src_c0, dst_sb, dst_c0, ident, waits,
               evac="dve"):
        i, pbank, guard = pool.next()
        tt = self.P.emit("pe", lambda e: e.transpose(
            pbank[:, 0:128], src_sb[:, src_c0:src_c0 + 128], ident[:]),
            waits=[waits, guard])
        if evac == "dve":
            tv = self.P.emit("dve", lambda e: e.tensor_copy(
                dst_sb[:, dst_c0:dst_c0 + 128], pbank[:, 0:128]),
                waits=[tt])
        else:
            tv = self.P.emit("act", lambda e: e.copy(
                dst_sb[:, dst_c0:dst_c0 + 128], pbank[:, 0:128]),
                waits=[tt])
        pool.consumed(i, tv)
        return tv

    def emit_mm_group(self, pool, mk_fn, waits, evac_fn, evac_eng="dve",
                      evac_waits=()):
        i, pbank, guard = pool.next()
        tm = self.P.emit("pe", lambda e, f=mk_fn, pb=pbank: f(e, pb),
                    waits=[waits, guard])
        tv = self.P.emit(evac_eng, lambda e, f=evac_fn, pb=pbank: f(e, pb),
                    waits=[tm, evac_waits])
        pool.consumed(i, tv)
        return tv



def build_nc():
    nc = bass.Bass("TRN2", target_bir_lowering=False,
                   detect_race_conditions=False)

    x_own = nc.dram_tensor("x_own", [TOK, H], F32, kind="ExternalInput")
    x_par = nc.dram_tensor("x_par", [TOK, H], F32, kind="ExternalInput")
    w_qkv = nc.dram_tensor("w_qkv", [3 * H, H], F32, kind="ExternalInput")
    bq_pp = nc.dram_tensor("bq_pp", [128, 48], F32, kind="ExternalInput")
    bv_row = nc.dram_tensor("bv_row", [1, H], F32, kind="ExternalInput")
    w_out = nc.dram_tensor("w_out", [H, H], F32, kind="ExternalInput")
    bo_row = nc.dram_tensor("bo_row", [1, H], F32, kind="ExternalInput")
    w_gate = nc.dram_tensor("w_gate", [II, H], F32, kind="ExternalInput")
    w_up = nc.dram_tensor("w_up", [II, H], F32, kind="ExternalInput")
    w_down = nc.dram_tensor("w_down", [H, II], F32, kind="ExternalInput")
    idf = nc.dram_tensor("idf", [128, 128], F32, kind="ExternalInput")
    idb = nc.dram_tensor("idb", [128, 128], BF16, kind="ExternalInput")
    out = nc.dram_tensor("out", [TOK, H], F32, kind="ExternalOutput")

    qiT_d = nc.dram_tensor("qiT_d", [16 * 128, 4 * TOK], BF16)
    ffn_d = nc.dram_tensor("ffn_d", [TOK, H], F32)
    wqkvT_d = nc.dram_tensor("wqkvT_d", [48 * 128, H], F32)
    woutT_d = nc.dram_tensor("woutT_d", [H, H], F32)
    K_d = nc.dram_tensor("K_d", [4 * 128, NHC * 512], F32)
    V_d = nc.dram_tensor("V_d", [4 * 128, 4 * H], F32)
    Q_d = nc.dram_tensor("Q_d", [2 * 128, NHC * 512], F32)

    es = ExitStack()
    with es:
        P = Plan(nc, es)
        sbg = lambda name, shape, dt: es.enter_context(
            nc.sbuf_tensor(name, shape, dt))

        sid_f = sbg("sid_f", [128, 128], F32)
        sid_b = sbg("sid_b", [128, 128], BF16)
        bq_sb = sbg("bq_sb", [128, 48], F32)
        ones1 = sbg("ones1", [1, 128], F32)
        sm_s = sbg("sm_s", [128, 8], F32)
        BL = Builder(nc, P, es)
        scope_ln = BL.scope_ln
        scope_q = BL.scope_q
        emit_ln = BL.emit_ln
        emit_round = BL.emit_round
        emit_quant = BL.emit_quant
        emit_T = BL.emit_T
        emit_mm_group = BL.emit_mm_group

        with nc.Block() as blk:
            P.emit("sp", lambda e: e.dma_start(bq_sb[:], bq_pp[:, :]),
                   inc=16, sem="s_misc")
            P.emit("sp", lambda e: e.dma_start(sid_f[:], idf[:, :]),
                   inc=16, sem="s_misc")
            P.emit("sp", lambda e: e.dma_start(sid_b[:], idb[:, :]),
                   inc=16, sem="s_misc")
            P.emit("dve", lambda e: e.memset(ones1[:], 1.0))
            P.flush(blk)

        scope_x = ExitStack()
        qxT = scope_x.enter_context(
            nc.sbuf_tensor("qxT", [128, NHC * TOK], BF16))
        if PHASES >= 2:
          with nc.Block() as blk:
            with (nc.psum_tensor("p1a", [128, 1024], BF16) as p1a,
                  nc.psum_tensor("p1b", [128, 1024], BF16) as p1b,
                  nc.sbuf_tensor("xn1", [128, H], F32) as xn1,
                  nc.sbuf_tensor("qx1", [128, H], BF16) as qx1):
                tp = PsumPool([p1a, p1b])
                prev_T = None
                for ti in range(8):
                    J = P.join()
                    tl = emit_ln(x_own, ti * 128, xn1, 0, head_waits=J)
                    tq1 = emit_quant(xn1, 0, qx1, 0, 2048, 3, [tl],
                                     dst_guard=prev_T)
                    for hc in range(NHC):
                        emit_T(tp, qx1, hc * 128, qxT, hc * TOK + ti * 128,
                               sid_b, [tq1], evac="act")
                    prev_T = P.mark("pe")
                P.flush(blk)

        if PHASES >= 3:
          with nc.Block() as blk:
            with (nc.psum_tensor("p2a", [128, 512], F32) as p2a,
                  nc.psum_tensor("p2b", [128, 512], F32) as p2b,
                  nc.psum_tensor("p2c", [128, 512], F32) as p2c,
                  nc.psum_tensor("p2t", [128, 1024], BF16) as p2t,
                  nc.psum_tensor("p2u", [128, 1024], BF16) as p2u,
                  nc.sbuf_tensor("wld0", [128, H], F32) as wld0,
                  nc.sbuf_tensor("wld1", [128, H], F32) as wld1,
                  nc.sbuf_tensor("qwn0", [128, H], BF16) as qwn0,
                  nc.sbuf_tensor("qwn1", [128, H], BF16) as qwn1,
                  nc.sbuf_tensor("qwgT", [128, NHC * 512], BF16) as qwgT,
                  nc.sbuf_tensor("qwuT", [128, NHC * 512], BF16) as qwuT,
                  nc.sbuf_tensor("gate", [128, 8 * 512], F32) as gate,
                  nc.sbuf_tensor("upb", [128, 8 * 512], F32) as upb,
                  nc.sbuf_tensor("qib", [128, 2048], BF16) as qib,
                  nc.sbuf_tensor("qiTb", [128, 4 * TOK], BF16) as qiTb):
                wldb = [wld0, wld1]
                qwnb = [qwn0, qwn1]
                tp = PsumPool([p2t, p2u])
                mp = PsumPool([p2a, p2b, p2c])
                J0 = P.join()
                tq_done = {-2: J0, -1: J0}
                tT_done = {-2: None, -1: None}
                t_mm_prev = None
                t_mul_prev = None
                kg = 0
                for ig in range(16):
                    for (wmat, slabT) in ((w_gate, qwgT), (w_up, qwuT)):
                        for ch in range(4):
                            b = kg % 2
                            r0 = ig * 512 + ch * 128
                            tl = P.emit("sp", lambda e, wmat=wmat, r0=r0,
                                        b=b: e.dma_start(
                                            wldb[b][:, 0:H],
                                            wmat[r0:r0 + 128, :]),
                                        waits=[tq_done[kg - 2]], inc=16,
                                        sem="s_wld")
                            tb = emit_quant(wldb[b], 0, qwnb[b], 0, 2048, 1,
                                            [tl],
                                            dst_guard=tT_done[kg - 2])
                            tq_done[kg] = tb
                            for hc in range(NHC):
                                emit_T(tp, qwnb[b], hc * 128, slabT,
                                       hc * 512 + ch * 128, sid_b,
                                       [tb, t_mm_prev],
                                       evac="act" if hc % 2 else "dve")
                            tT_done[kg] = P.mark("pe")
                            kg += 1
                    t_slab = [P.mark("dve"), P.mark("act")]
                    for (slabT, dst) in ((qwgT, gate), (qwuT, upb)):
                        for tc in range(8):
                            def mk(e, pb, slabT=slabT, tc=tc):
                                for hc in range(NHC):
                                    last = e.matmul(
                                        pb[:, 0:512],
                                        qxT[:, hc * TOK + tc * 128:
                                            hc * TOK + tc * 128 + 128],
                                        slabT[:, hc * 512:(hc + 1) * 512],
                                        start=(hc == 0), stop=(hc == NHC - 1))
                                return last
                            def ev(e, pb, dst=dst, tc=tc):
                                return e.copy(
                                    dst[:, tc * 512:(tc + 1) * 512],
                                    pb[:, 0:512])
                            emit_mm_group(mp, mk, [t_slab], ev, "act",
                                          evac_waits=[t_mul_prev])
                    t_mm_prev = P.mark("pe")
                    t_gu = P.mark("act")
                    t_sil = []
                    for tc in range(8):
                        t_sil.append(P.emit(
                            "act", lambda e, tc=tc: e.activation(
                                gate[:, tc * 512:(tc + 1) * 512],
                                gate[:, tc * 512:(tc + 1) * 512], AF.Silu),
                            waits=[t_gu]))
                    t_mul = []
                    for tc in range(8):
                        t_mul.append(P.emit(
                            "dve", lambda e, tc=tc: e.tensor_tensor(
                                gate[:, tc * 512:(tc + 1) * 512],
                                gate[:, tc * 512:(tc + 1) * 512],
                                upb[:, tc * 512:(tc + 1) * 512], ALU.mult),
                            waits=[t_sil[tc]]))
                    t_store_prev = P.mark("t_qit")
                    for tcg in range(2):
                        t_q = emit_quant(gate, tcg * 2048, qib, 0, 2048, 3,
                                         [t_mul[tcg * 4:tcg * 4 + 4]],
                                         dst_guard=P.mark("pe"))
                        for tcl in range(4):
                            tc = tcg * 4 + tcl
                            for q4 in range(4):
                                emit_T(tp, qib, tcl * 512 + q4 * 128, qiTb,
                                       q4 * TOK + tc * 128, sid_b,
                                       [t_q, t_store_prev],
                                       evac="act" if q4 % 2 else "dve")
                    t_mul_prev = P.mark("dve")
                    t_qiT = [P.mark("dve"), P.mark("act")]
                    P.emit("pl", lambda e, ig=ig: e.dma_start(
                        qiT_d[ig * 128:(ig + 1) * 128, :],
                        qiTb[:, 0:4 * TOK]), waits=t_qiT, inc=16,
                        sem="t_qit")
                P.flush(blk)
        scope_x.close()

        if PHASES >= 4:
          with nc.Block() as blk:
            with (nc.sbuf_tensor("wdld", [128, II], F32) as wdld,
                  nc.sbuf_tensor("qwdn", [128, II], BF16) as qwdn,
                  nc.sbuf_tensor("qwdT", [128, 64 * 512], BF16) as qwdT,
                  nc.sbuf_tensor("qiS0", [128, 4 * TOK], BF16) as qiS0,
                  nc.sbuf_tensor("qiS1", [128, 4 * TOK], BF16) as qiS1,
                  nc.sbuf_tensor("fout", [128, 512], F32) as fout):
                qiSb = [qiS0, qiS1]
                for hg in range(4):
                    J = P.join()
                    with (nc.psum_tensor(f"p3t_{hg}", [128, 1024], BF16)
                          as p3t,
                          nc.psum_tensor(f"p3u_{hg}", [128, 1024], BF16)
                          as p3u):
                        tp = PsumPool([p3t, p3u])
                        t_wread = J
                        tT_prev = None
                        for ch in range(4):
                            r0 = hg * 512 + ch * 128
                            tl = P.emit("sp", lambda e, r0=r0: e.dma_start(
                                wdld[:, 0:II], w_down[r0:r0 + 128, :]),
                                waits=t_wread, inc=16, sem="s_wdld")
                            tprev = tl
                            for sc in range(4):
                                tprev = emit_quant(wdld, sc * 2048, qwdn,
                                                   sc * 2048, 2048, 1,
                                                   [tprev],
                                                   dst_guard=tT_prev)
                            t_wread = [tprev]
                            for ic in range(64):
                                emit_T(tp, qwdn, ic * 128, qwdT,
                                       ic * 512 + ch * 128, sid_b, [tprev],
                                       evac="act" if ic % 2 else "dve")
                            tT_prev = P.mark("pe")
                        P.flush(blk)
                    t_slab = [P.mark("dve"), P.mark("act")]
                    with ExitStack() as ps8:
                        pd = [ps8.enter_context(nc.psum_tensor(
                            f"pd{i}_{hg}", [128, 512], F32))
                            for i in range(8)]
                        t_mmsl = {-2: None, -1: None}
                        for sl in range(16):
                            bq = sl % 2
                            tq = P.emit("sp", lambda e, sl=sl, bq=bq:
                                        e.dma_start(
                                            qiSb[bq][:, 0:4 * TOK],
                                            qiT_d[sl * 128:(sl + 1) * 128,
                                                  :]),
                                        waits=[t_slab, t_mmsl[sl - 2]],
                                        inc=16, sem="s_qis")
                            for tc in range(8):
                                def mk(e, sl=sl, tc=tc, bq=bq):
                                    for i4 in range(4):
                                        ic = sl * 4 + i4
                                        last = e.matmul(
                                            pd[tc][:, 0:512],
                                            qiSb[bq][:, i4 * TOK + tc * 128:
                                                i4 * TOK + tc * 128 + 128],
                                            qwdT[:, ic * 512:(ic + 1) * 512],
                                            start=(ic == 0), stop=(ic == 63))
                                    return last
                                P.emit("pe", lambda e, f=mk: f(e), waits=[tq])
                            t_mmsl[sl] = P.mark("pe")
                        t_mm = P.mark("pe")
                        t_fread = None
                        for tc in range(8):
                            tev = P.emit("act", lambda e, tc=tc: e.copy(
                                fout[:, 0:512], pd[tc][:, 0:512]),
                                waits=[t_mm, t_fread])
                            t_fread = P.emit(
                                "pl", lambda e, tc=tc, hg=hg: e.dma_start(
                                    ffn_d[tc * 128:(tc + 1) * 128,
                                          hg * 512:(hg + 1) * 512],
                                    fout[:, 0:512]), waits=[tev], inc=16,
                                sem="t_ffn")
                        P.flush(blk)
        scope_q.close()

        if PHASES >= 5:
          with nc.Block() as blk:
            with (nc.psum_tensor("p4a", [128, 512], F32) as p4a,
                  nc.psum_tensor("p4b", [128, 512], F32) as p4b,
                  nc.sbuf_tensor("wrow", [128, H], F32) as wrow,
                  nc.sbuf_tensor("wcol", [128, NHC * 128], F32) as wcol):
                tp = PsumPool([p4a, p4b])
                t_rd = None
                t_st = None
                for rc in range(48):
                    tl = P.emit("sp", lambda e, rc=rc: e.dma_start(
                        wrow[:, 0:H], w_qkv[rc * 128:(rc + 1) * 128, :]),
                        waits=[t_rd], inc=16, sem="s_wrow")
                    for hc in range(NHC):
                        emit_T(tp, wrow, hc * 128, wcol, hc * 128,
                               sid_f, [tl, t_st],
                               evac="act" if hc % 2 else "dve")
                    t_rd = P.mark("pe")
                    t_c = [P.mark("dve"), P.mark("act")]
                    t_st = P.emit("pl", lambda e, rc=rc: e.dma_start(
                        wqkvT_d[rc * 128:(rc + 1) * 128, :],
                        wcol[:, 0:H]), waits=[t_c], inc=16, sem="t_wT")
                for rc in range(16):
                    tl = P.emit("sp", lambda e, rc=rc: e.dma_start(
                        wrow[:, 0:H], w_out[rc * 128:(rc + 1) * 128, :]),
                        waits=[t_rd], inc=16, sem="s_wrow")
                    for hc in range(NHC):
                        emit_T(tp, wrow, hc * 128, wcol, hc * 128,
                               sid_f, [tl, t_st],
                               evac="act" if hc % 2 else "dve")
                    t_rd = P.mark("pe")
                    t_c = [P.mark("dve"), P.mark("act")]
                    for hc in range(NHC):
                        t_st = P.emit(
                            "pl", lambda e, rc=rc, hc=hc: e.dma_start(
                                woutT_d[hc * 128:(hc + 1) * 128,
                                        rc * 128:(rc + 1) * 128],
                                wcol[:, hc * 128:(hc + 1) * 128]),
                            waits=[t_c], inc=16, sem="t_wT")
                P.flush(blk)

        if PHASES >= 6:
          with nc.Block() as blk:
            with (nc.psum_tensor("p5a", [128, 512], F32) as p5a,
                  nc.psum_tensor("p5b", [128, 512], F32) as p5b,
                  nc.psum_tensor("p5c", [128, 512], F32) as p5c,
                  nc.psum_tensor("p5t", [128, 512], F32) as p5t,
                  nc.psum_tensor("p5u", [128, 512], F32) as p5u,
                  nc.sbuf_tensor("xnT", [128, NHC * 512], F32) as xnT,
                  nc.sbuf_tensor("xn5", [128, H], F32) as xn5,
                  nc.sbuf_tensor("wcolK", [128, H], F32) as wcolK,
                  nc.sbuf_tensor("wcolK1", [128, H], F32) as wcolK1,
                  nc.sbuf_tensor("kv_sb", [128, NHC * 512], F32) as kv_sb,
                  nc.sbuf_tensor("bvt", [128, 512], F32) as bvt):
                tp = PsumPool([p5t, p5u])
                mp = PsumPool([p5a, p5b, p5c])

                for kt in range(4):
                    J = P.join()
                    src = x_own if kt < 2 else x_par
                    t_prev = [J]
                    for st4 in range(4):
                        row0 = (kt * 512 + st4 * 128) % TOK
                        tl = emit_ln(src, row0, xn5, 0, head_waits=t_prev)
                        for hc in range(NHC):
                            emit_T(tp, xn5, hc * 128, xnT,
                                   hc * 512 + st4 * 128, sid_f, [tl])
                        t_prev = [P.mark("pe"), P.mark("dve")]
                    t_xnT = P.mark("dve")

                    wcb = [wcolK, wcolK1]
                    wk_hist = {-2: [J], -1: [J]}
                    t_kst = []
                    for oc in range(16):
                        bw = oc % 2
                        tw = P.emit("sp", lambda e, oc=oc, bw=bw: e.dma_start(
                            wcb[bw][:, 0:H],
                            wqkvT_d[(16 + oc) * 128:(17 + oc) * 128, :]),
                            waits=wk_hist[oc - 2], inc=16, sem="s_wk")
                        def mkK(e, pb, bw=bw):
                            for hc in range(NHC):
                                last = e.matmul(
                                    pb[:, 0:512],
                                    wcb[bw][:, hc * 128:(hc + 1) * 128],
                                    xnT[:, hc * 512:(hc + 1) * 512],
                                    start=(hc == 0), stop=(hc == NHC - 1))
                            return last
                        def evK(e, pb, oc=oc):
                            return e.tensor_single_scalar(
                                kv_sb[:, oc * 512:(oc + 1) * 512],
                                pb[:, 0:512], bq_sb[:, 16 + oc:17 + oc],
                                ALU.add)
                        emit_mm_group(mp, mkK, [tw, t_xnT], evK, "dve")
                        wk_hist[oc] = [P.mark("pe")]
                    t_k = P.mark("dve")
                    t_kst = [P.emit("pl", lambda e, kt=kt: e.dma_start(
                        K_d[kt * 128:(kt + 1) * 128, :],
                        kv_sb[:, 0:NHC * 512]), waits=[t_k], inc=16,
                        sem="t_K")]

                    if kt < 2:
                        wk_hist = {0: [wk_hist[14], t_kst],
                                   1: [wk_hist[15], t_kst]}
                        for oc in range(16):
                            bw = oc % 2
                            tw = P.emit("sp", lambda e, oc=oc, bw=bw:
                                        e.dma_start(
                                            wcb[bw][:, 0:H],
                                            wqkvT_d[oc * 128:
                                                    (oc + 1) * 128, :]),
                                        waits=wk_hist[oc - 2 if oc >= 2
                                                      else oc],
                                        inc=16, sem="s_wk")
                            def mkQ(e, pb, bw=bw):
                                for hc in range(NHC):
                                    last = e.matmul(
                                        pb[:, 0:512],
                                        wcb[bw][:, hc * 128:(hc + 1) * 128],
                                        xnT[:, hc * 512:(hc + 1) * 512],
                                        start=(hc == 0),
                                        stop=(hc == NHC - 1))
                                return last
                            def evQ(e, pb, oc=oc):
                                return e.tensor_scalar(
                                    kv_sb[:, oc * 512:(oc + 1) * 512],
                                    pb[:, 0:512], bq_sb[:, oc:oc + 1],
                                    RSQH, ALU.add, ALU.mult)
                            emit_mm_group(mp, mkQ, [tw], evQ, "dve")
                            wk_hist[oc] = [P.mark("pe")]
                        t_q = P.mark("dve")
                        t_kst = [P.emit("pl", lambda e, kt=kt: e.dma_start(
                            Q_d[kt * 128:(kt + 1) * 128, :],
                            kv_sb[:, 0:NHC * 512]), waits=[t_q], inc=16,
                            sem="t_Q")]

                    wk_hist = {0: [wk_hist[14], t_kst],
                               1: [wk_hist[15], t_kst]}
                    t_stage = None
                    for rc in range(16):
                        bw = rc % 2
                        tw = P.emit("sp", lambda e, rc=rc, bw=bw:
                                    e.dma_start(
                                        wcb[bw][:, 0:H],
                                        wqkvT_d[(32 + rc) * 128:
                                                (33 + rc) * 128, :]),
                                    waits=wk_hist[rc - 2 if rc >= 2 else rc],
                                    inc=16, sem="s_wk")
                        def mkV(e, pb, bw=bw):
                            for hc in range(NHC):
                                last = e.matmul(
                                    pb[:, 0:512],
                                    wcb[bw][:, hc * 128:(hc + 1) * 128],
                                    xnT[:, hc * 512:(hc + 1) * 512],
                                    start=(hc == 0), stop=(hc == NHC - 1))
                            return last
                        i, pb_, guard = mp.next()
                        tm = P.emit("pe", lambda e, f=mkV, pb=pb_: f(e, pb),
                                    waits=[tw, t_xnT, guard])
                        tvv = P.emit("dve", lambda e, pb=pb_, rc=rc:
                                     e.tensor_single_scalar(
                                         bvt[:, 0:512], pb[:, 0:512],
                                         bq_sb[:, 32 + rc:33 + rc], ALU.add),
                                     waits=[tm, t_stage])
                        mp.consumed(i, tvv)
                        for kc in range(4):
                            emit_T(tp, bvt, kc * 128, kv_sb,
                                   kc * 2048 + rc * 128, sid_f,
                                   [tvv, t_kst])
                        t_stage = P.mark("pe")
                        wk_hist[rc] = [P.mark("pe")]
                    t_v = P.mark("dve")
                    t_kst = [P.emit("pl", lambda e, kt=kt: e.dma_start(
                        V_d[kt * 128:(kt + 1) * 128, :],
                        kv_sb[:, 0:4 * H]), waits=[t_v], inc=16,
                        sem="t_V")]
                    P.emit("dve", None, waits=[t_kst], inc=0)
                P.flush(blk)
        scope_ln.close()

        if PHASES >= 7:
          with nc.Block() as blk:
            with ExitStack() as es6:
                ec6 = es6.enter_context
                probsT = ec6(nc.sbuf_tensor("probsT", [128, 16 * 512], F32))
                ctxT = ec6(nc.sbuf_tensor("ctxT", [128, 16 * 512], F32))
                ctx = ec6(nc.sbuf_tensor("ctx", [128, 4 * H], F32))
                bot = ec6(nc.sbuf_tensor("bot", [128, H], F32))
                rs4 = ec6(nc.sbuf_tensor("rs4", [128, 4], F32))
                pbuf = ec6(nc.sbuf_tensor("pbuf", [128, S], F32))
                scs1 = ec6(nc.sbuf_tensor("scs1", [128, S], F32))
                scs1b = ec6(nc.sbuf_tensor("scs1b", [128, S], F32))
                KVs = ec6(nc.sbuf_tensor("KVs", [128, 16 * 512], F32))
                p6a = ec6(nc.psum_tensor("p6a", [128, 512], F32))
                p6b = ec6(nc.psum_tensor("p6b", [128, 512], F32))
                p6t = ec6(nc.psum_tensor("p6t", [128, 512], F32))
                p6u = ec6(nc.psum_tensor("p6u", [128, 512], F32))
                mp = PsumPool([p6a, p6b])
                tp = PsumPool([p6t, p6u])
                tbo = P.emit("sp", lambda e: e.dma_start(
                    pbuf[0:1, 0:H], bo_row[:, :]), inc=16, sem="s_misc")
                for og in range(4):
                    def mkb(e, pb, og=og):
                        return e.matmul(pb[:, 0:512], ones1[:],
                                        pbuf[0:1, og * 512:(og + 1) * 512],
                                        start=True, stop=True)
                    def evb(e, pb, og=og):
                        return e.tensor_copy(
                            bot[:, og * 512:(og + 1) * 512], pb[:, 0:512])
                    emit_mm_group(mp, mkb, [tbo], evb, "dve")
                t_bo = P.mark("dve")

                for qh in range(2):
                    J = P.join()
                    with nc.sbuf_tensor(f"Qh_{qh}", [128, NHC * 512],
                                        F32) as Qh:
                        tlq = P.emit("sp", lambda e, qh=qh: e.dma_start(
                            Qh[:, 0:NHC * 512],
                            Q_d[qh * 128:(qh + 1) * 128, :]),
                            waits=[J], inc=16, sem="s_qh")
                        t_kread = [J]
                        scsb = [scs1, scs1b]
                        t2h = {-2: None, -1: None}
                        for qc in range(4):
                            sb6 = scsb[qc % 2]
                            for kt in range(4):
                                tlk = P.emit("sp", lambda e, kt=kt:
                                             e.dma_start(
                                                 KVs[:, 0:NHC * 512],
                                                 K_d[kt * 128:
                                                     (kt + 1) * 128, :]),
                                             waits=t_kread, inc=16,
                                             sem="s_kvs")
                                def mkS(e, pb, qc=qc):
                                    for oc in range(16):
                                        last = e.matmul(
                                            pb[:, 0:512],
                                            Qh[:, oc * 512 + qc * 128:
                                               oc * 512 + qc * 128 + 128],
                                            KVs[:, oc * 512:(oc + 1) * 512],
                                            start=(oc == 0), stop=(oc == 15))
                                    return last
                                def evS(e, pb, kt=kt, sb6=sb6):
                                    return e.tensor_copy(
                                        sb6[:, kt * 512:(kt + 1) * 512],
                                        pb[:, 0:512])
                                emit_mm_group(mp, mkS, [tlq, tlk], evS,
                                              "dve",
                                              evac_waits=[t2h[qc - 2]])
                                t_kread = [P.mark("pe")]
                            t_sc = P.mark("dve")
                            t1 = P.emit("dve", lambda e, qc=qc, sb6=sb6:
                                        e.reduce_max(
                                            sm_s[:, qc:qc + 1], sb6[:, 0:S],
                                            axis=AX.X, negate=True),
                                        waits=[t_sc])
                            t2 = P.emit("act", lambda e, qc=qc, sb6=sb6:
                                        e.activation(
                                            pbuf[:, 0:S], sb6[:, 0:S],
                                            AF.Exp, bias=sm_s[:, qc:qc + 1],
                                            accum_out=sm_s[:, 4 + qc:5 + qc]),
                                        waits=[t1, P.mark("pe")])
                            t2h[qc] = t2
                            P.emit("dve", lambda e, qc=qc: e.reciprocal(
                                rs4[:, qc:qc + 1], sm_s[:, 4 + qc:5 + qc]),
                                waits=[t2])
                            for kc in range(16):
                                emit_T(tp, pbuf, kc * 128, probsT,
                                       kc * 512 + qc * 128, sid_f, [t2])
                            t_kread = [P.mark("pe")]
                        t_probs = P.mark("dve")
                        tz = []
                        for qc in range(4):
                            tz.append(P.emit("dve", lambda e, qc=qc:
                                             e.memset(
                                                 ctx[:, qc * H:(qc + 1) * H],
                                                 0.0), waits=[J]))
                        t_vread = [P.mark("pe")]
                        for kt in range(4):
                            tlv = P.emit("sp", lambda e, kt=kt: e.dma_start(
                                KVs[:, 0:4 * H],
                                V_d[kt * 128:(kt + 1) * 128, :]),
                                waits=t_vread, inc=16, sem="s_kvs")
                            for qc in range(4):
                                for vg in range(4):
                                    def mkC(e, pb, kt=kt, qc=qc, vg=vg):
                                        for kc in range(4):
                                            kcg = kt * 4 + kc
                                            last = e.matmul(
                                                pb[:, 0:512],
                                                probsT[:,
                                                       kcg * 512 + qc * 128:
                                                       kcg * 512 + qc * 128
                                                       + 128],
                                                KVs[:, kc * 2048 + vg * 512:
                                                    kc * 2048 +
                                                    (vg + 1) * 512],
                                                start=(kc == 0),
                                                stop=(kc == 3))
                                        return last
                                    def evC(e, pb, qc=qc, vg=vg):
                                        return e.scalar_tensor_tensor(
                                            ctx[:, qc * H + vg * 512:
                                                qc * H + (vg + 1) * 512],
                                            pb[:, 0:512], rs4[:, qc:qc + 1],
                                            ctx[:, qc * H + vg * 512:
                                                qc * H + (vg + 1) * 512],
                                            ALU.mult, ALU.add)
                                    emit_mm_group(mp, mkC,
                                                  [tlv, t_probs, tz[qc]],
                                                  evC, "dve")
                            t_vread = [P.mark("pe")]
                        t_ctx = P.mark("dve")
                        for qc in range(4):
                            for vc in range(16):
                                emit_T(tp, ctx, qc * H + vc * 128, ctxT,
                                       vc * 512 + qc * 128, sid_f, [t_ctx])
                        t_ctxT = P.mark("dve")
                        P.flush(blk)
                    with ExitStack() as ps4:
                        wot = ps4.enter_context(nc.sbuf_tensor(
                            f"wot_{qh}", [128, 2 * 512], F32))
                        xf = ps4.enter_context(nc.sbuf_tensor(
                            f"xf_{qh}", [128, 512], F32))
                        ff = ps4.enter_context(nc.sbuf_tensor(
                            f"ff_{qh}", [128, 512], F32))
                        oo = ps4.enter_context(nc.sbuf_tensor(
                            f"oo_{qh}", [128, 512], F32))
                        po = [ps4.enter_context(nc.psum_tensor(
                            f"po{i}_{qh}", [128, 512], F32))
                            for i in range(4)]
                        t_oo = None
                        tws = [None, None]
                        for og in range(4):
                            t_ev_prev = P.mark("dve")
                            for vc in range(16):
                                par = vc % 2
                                tw = P.emit(
                                    "sp",
                                    lambda e, vc=vc, og=og, par=par:
                                    e.dma_start(
                                        wot[:, par * 512:(par + 1) * 512],
                                        woutT_d[vc * 128:(vc + 1) * 128,
                                                og * 512:(og + 1) * 512]),
                                    waits=[tws[par], t_ctxT], inc=16,
                                    sem=f"s_wot{par}")
                                def mko(e, vc=vc, par=par):
                                    for qc in range(4):
                                        last = e.matmul(
                                            po[qc][:, 0:512],
                                            ctxT[:, vc * 512 + qc * 128:
                                                 vc * 512 + qc * 128 + 128],
                                            wot[:, par * 512:
                                                (par + 1) * 512],
                                            start=(vc == 0), stop=(vc == 15))
                                    return last
                                tm = P.emit("pe", lambda e, f=mko: f(e),
                                            waits=[tw, t_ctxT, t_ev_prev])
                                tws[par] = tm
                            t_mm = P.mark("pe")
                            for qc in range(4):
                                row0 = qh * 512 + qc * 128
                                tx = P.emit("sp",
                                            lambda e, row0=row0, og=og:
                                            e.dma_start(
                                                xf[:, 0:512],
                                                x_own[row0:row0 + 128,
                                                      og * 512:
                                                      (og + 1) * 512]),
                                            waits=[t_oo], inc=16,
                                            sem="s_xf")
                                tf = P.emit("sp",
                                            lambda e, row0=row0, og=og:
                                            e.dma_start(
                                                ff[:, 0:512],
                                                ffn_d[row0:row0 + 128,
                                                      og * 512:
                                                      (og + 1) * 512]),
                                            waits=[t_oo], inc=16,
                                            sem="s_ff")
                                P.emit("dve", lambda e, qc=qc:
                                       e.tensor_tensor(
                                           oo[:, 0:512], po[qc][:, 0:512],
                                           xf[:, 0:512], ALU.add),
                                       waits=[t_mm, tx, t_oo])
                                P.emit("dve", lambda e: e.tensor_tensor(
                                    oo[:, 0:512], oo[:, 0:512],
                                    ff[:, 0:512], ALU.add), waits=[tf])
                                t3 = P.emit("dve", lambda e, og=og:
                                            e.tensor_tensor(
                                                oo[:, 0:512], oo[:, 0:512],
                                                bot[:, og * 512:
                                                    (og + 1) * 512],
                                                ALU.add), waits=[t_bo])
                                t_oo = P.emit("pl",
                                              lambda e, row0=row0, og=og:
                                              e.dma_start(
                                                  out[row0:row0 + 128,
                                                      og * 512:
                                                      (og + 1) * 512],
                                                  oo[:, 0:512]),
                                              waits=[t3], inc=16,
                                              sem="t_out")
                        P.flush(blk)

        with nc.Block() as blk:
            P.emit("pl", None, waits=P.join(), inc=0, allow_self=True)
            P.flush(blk)

    return nc


_NC_CACHE = None


def kernel(L0_attn_hidden_states, L0_attn_w_qkv, L0_attn_b_qkv, L0_attn_w_out,
           L0_attn_b_out, L0_ffn_w_gate, L0_ffn_w_up, L0_ffn_w_down):
    global _NC_CACHE
    x = np.ascontiguousarray(np.asarray(L0_attn_hidden_states,
                                        dtype=np.float32))
    w_qkv_a = np.ascontiguousarray(np.asarray(L0_attn_w_qkv,
                                              dtype=np.float32))
    b_qkv = np.asarray(L0_attn_b_qkv, dtype=np.float32)
    w_out_a = np.ascontiguousarray(np.asarray(L0_attn_w_out,
                                              dtype=np.float32))
    b_out = np.asarray(L0_attn_b_out, dtype=np.float32)
    w_gate_a = np.ascontiguousarray(np.asarray(L0_ffn_w_gate,
                                               dtype=np.float32))
    w_up_a = np.ascontiguousarray(np.asarray(L0_ffn_w_up, dtype=np.float32))
    w_down_a = np.ascontiguousarray(np.asarray(L0_ffn_w_down,
                                               dtype=np.float32))

    bq_pp = np.ascontiguousarray(b_qkv.reshape(48, 128).T)
    bv_row = np.ascontiguousarray(b_qkv[2 * H:3 * H].reshape(1, H))
    bo_row = np.ascontiguousarray(b_out.reshape(1, H))
    idf = np.eye(128, dtype=np.float32)
    idb = np.eye(128, dtype=ml_dtypes.bfloat16)

    if _NC_CACHE is None:
        _NC_CACHE = build_nc()
    nc = _NC_CACHE

    in_maps = []
    for c in range(8):
        b, h = c // 2, c % 2
        xo = np.ascontiguousarray(x[b, h * 1024:(h + 1) * 1024, :])
        xp = np.ascontiguousarray(x[b, (1 - h) * 1024:(2 - h) * 1024, :])
        in_maps.append({
            "x_own": xo, "x_par": xp,
            "w_qkv": w_qkv_a, "bq_pp": bq_pp, "bv_row": bv_row,
            "w_out": w_out_a, "bo_row": bo_row,
            "w_gate": w_gate_a, "w_up": w_up_a, "w_down": w_down_a,
            "idf": idf, "idb": idb,
        })

    res = run_bass_kernel_spmd(nc, in_maps, core_ids=list(range(8)))
    outp = np.empty((B, S, H), dtype=np.float32)
    for c in range(8):
        b, h = c // 2, c % 2
        outp[b, h * 1024:(h + 1) * 1024, :] = res.results[c]["out"]
    return outp


# version 10
# speedup vs baseline: 1.0599x; 1.0599x over previous
import numpy as np
import ml_dtypes
from contextlib import ExitStack

import concourse.bass as bass
import concourse.mybir as mybir
from concourse.bass_utils import run_bass_kernel_spmd

F32 = mybir.dt.float32
BF16 = mybir.dt.bfloat16
I32 = mybir.dt.int32
ALU = mybir.AluOpType
AF = mybir.ActivationFunctionType
AX = mybir.AxisListType

B, S, H, II = 4, 2048, 2048, 8192
TOK = 1024
NHC = H // 128
LN2 = float(np.log(2.0))
INVLN2 = float(1.0 / np.log(2.0))
B52 = 6755399441055744.0
B23 = 12582912.0
RSQH = float(1.0 / np.sqrt(np.float64(H)))
EPS = 1e-5

PHASES = 7

ENGINES = ("sp", "pe", "dve", "act", "pl")
ENG_ATTR = {"sp": "sync", "pe": "tensor", "dve": "vector", "act": "scalar",
            "pl": "gpsimd"}


def _flat(ws):
    out = {}
    stack = list(ws)
    while stack:
        w = stack.pop()
        if w is None:
            continue
        if isinstance(w, (list, tuple)) and not (
                len(w) == 2 and isinstance(w[0], str)):
            stack.extend(w)
        else:
            eng, v = w
            if out.get(eng, -1) < v:
                out[eng] = v
    return [(e, v) for e, v in out.items()]


class Plan:
    def __init__(self, nc, es):
        self.nc = nc
        self.es = es
        self.sem = {e: es.enter_context(nc.semaphore("sem_" + e))
                    for e in ENGINES}
        self.cnt = {e: 0 for e in ENGINES}
        self.q = {e: [] for e in ENGINES}

    def _sem(self, name):
        if name not in self.sem:
            self.sem[name] = self.es.enter_context(
                self.nc.semaphore("sem_" + name))
            self.cnt[name] = 0
        return self.sem[name]

    def emit(self, eng, fn, waits=(), inc=1, allow_self=False, sem=None):
        s = eng if sem is None else sem
        self._sem(s)
        waits = _flat(waits)
        if eng in ("dve", "act") and fn is not None:
            waits = _flat([waits, (eng, self.cnt[eng])])
            allow_self = True
        self.cnt[s] += inc
        sems = self.sem

        def run(e, fn=fn, waits=tuple(waits), inc=inc, eng=eng, s=s,
                allow_self=allow_self):
            for (we, wv) in waits:
                if we != eng or allow_self:
                    e.wait_ge(sems[we], wv)
            if fn is not None:
                inst = fn(e)
                if inc:
                    inst.then_inc(sems[s], inc)

        self.q[eng].append(run)
        return (s, self.cnt[s])

    def mark(self, eng):
        self._sem(eng)
        return (eng, self.cnt[eng])

    def join(self):
        return [(k, v) for k, v in self.cnt.items()]

    def flush(self, blk):
        for eng in ENGINES:
            if self.q[eng]:
                lst = self.q[eng]
                self.q[eng] = []
                getattr(blk, ENG_ATTR[eng])(
                    lambda e, lst=lst: [r(e) for r in lst])


class PsumPool:
    def __init__(self, banks):
        self.banks = banks
        self.guard = [None] * len(banks)
        self.i = 0

    def next(self):
        i = self.i % len(self.banks)
        self.i += 1
        return i, self.banks[i], self.guard[i]

    def consumed(self, i, tok):
        self.guard[i] = tok



class Builder:

    def __init__(self, nc, P, es):
        self.nc = nc
        self.P = P
        sbg = lambda name, shape, dt: es.enter_context(
            nc.sbuf_tensor(name, shape, dt))
        self.ln_s = sbg("ln_s", [128, 8], F32)
        self.qs = [sbg(f"qs{i}", [128, 64], F32) for i in range(8)]
        self.qind = sbg("qind", [128, 4], F32)
        self._ind_done = False
        self.scope_ln = ExitStack()
        self.ln_xc = self.scope_ln.enter_context(
            nc.sbuf_tensor("ln_xc", [128, H], F32))
        self.ln_sq = self.scope_ln.enter_context(
            nc.sbuf_tensor("ln_sq", [128, H], F32))
        self.scope_q = ExitStack()
        self.qf = [self.scope_q.enter_context(
            nc.sbuf_tensor(f"qf{i}", [128, 2048], F32)) for i in range(4)]

    def emit_ln(self, x_dram, row0, dst_sb, dst_c0, head_waits=()):
        t_ld = self.P.emit("sp", lambda e: e.dma_start(
            self.ln_xc[:, 0:H], x_dram[row0:row0 + 128, :]),
            waits=head_waits, inc=16, sem="s_ln")
        self.P.emit("dve", lambda e: e.reduce_sum(
            self.ln_s[:, 0:1], self.ln_xc[:, 0:H], axis=AX.X), waits=[t_ld])
        self.P.emit("dve", lambda e: e.tensor_scalar_mul(
            self.ln_s[:, 1:2], self.ln_s[:, 0:1], -1.0 / H))
        t3 = self.P.emit("dve", lambda e: e.tensor_scalar_add(
            self.ln_xc[:, 0:H], self.ln_xc[:, 0:H], self.ln_s[:, 1:2]))
        t4 = self.P.emit("act", lambda e: e.activation(
            self.ln_sq[:, 0:H], self.ln_xc[:, 0:H], AF.Square,
            accum_out=self.ln_s[:, 2:3]), waits=[t3])
        t5 = self.P.emit("dve", lambda e: e.tensor_scalar(
            self.ln_s[:, 3:4], self.ln_s[:, 2:3], 1.0 / H, EPS, ALU.mult, ALU.add),
            waits=[t4])
        t6 = self.P.emit("act", lambda e: e.activation(
            self.ln_s[:, 4:5], self.ln_s[:, 3:4], AF.Sqrt), waits=[t5])
        self.P.emit("dve", lambda e: e.reciprocal(
            self.ln_s[:, 5:6], self.ln_s[:, 4:5]), waits=[t6])
        self.P.emit("dve", lambda e: e.tensor_mul(
            self.ln_s[:, 6:7], self.ln_s[:, 5:6], self.ln_s[:, 5:6]))
        self.P.emit("dve", lambda e: e.tensor_mul(
            self.ln_s[:, 7:8], self.ln_s[:, 6:7], self.ln_s[:, 3:4]))
        self.P.emit("dve", lambda e: e.tensor_scalar(
            self.ln_s[:, 6:7], self.ln_s[:, 7:8], -0.5, 1.5, ALU.mult, ALU.add))
        self.P.emit("dve", lambda e: e.tensor_mul(
            self.ln_s[:, 7:8], self.ln_s[:, 5:6], self.ln_s[:, 6:7]))
        tw = self.P.emit("dve", lambda e: e.tensor_scalar_mul(
            dst_sb[:, dst_c0:dst_c0 + H], self.ln_xc[:, 0:H], self.ln_s[:, 7:8]))
        return tw

    def emit_ind(self):
        if self._ind_done:
            return
        self._ind_done = True
        self.P.emit("dve", lambda e: e.memset(self.qind[:, 0:1], 1.0))
        self.P.emit("dve", lambda e: e.tensor_scalar(
            self.qind[:, 0:1], self.qind[:, 0:1], B52, -B52,
            ALU.add, ALU.add))
        self.P.emit("dve", lambda e: e.tensor_scalar(
            self.qind[:, 1:2], self.qind[:, 0:1], -1.0, 1.0,
            ALU.mult, ALU.add))
        self.P.emit("dve", lambda e: e.tensor_scalar_mul(
            self.qind[:, 2:3], self.qind[:, 1:2], B23))
        self.P.emit("dve", lambda e: e.scalar_tensor_tensor(
            self.qind[:, 2:3], self.qind[:, 0:1], B52, self.qind[:, 2:3],
            ALU.mult, ALU.add))

    def emit_round(self, t_ap, r_ap, tmp_ap, waits):
        self.emit_ind()
        self.P.emit("dve", lambda e: e.tensor_scalar(
            tmp_ap, t_ap, B23, -B23, ALU.add, ALU.add), waits=waits)
        self.P.emit("dve", lambda e: e.tensor_scalar(
            r_ap, t_ap, B52, -B52, ALU.add, ALU.add))
        tr = self.P.emit("dve", lambda e: e.scalar_tensor_tensor(
            r_ap, tmp_ap, self.qind[:, 1:2], r_ap, ALU.mult, ALU.add))
        return tr

    def emit_quant(self, src_sb, src_c0, dst_sb, dst_c0, F, m, waits,
                   dst_guard=None):
        nb = F // 32
        cmax = (7.5 if m == 3 else 6.0) * (2.0 ** m)
        seg = lambda t, c0: t[:, c0:c0 + F].rearrange(
            "p (b c) -> p b c", c=32)
        bc = lambda t: t[:, 0:nb].unsqueeze(2).broadcast_to([128, nb, 32])
        self.emit_ind()
        P, qs, qf = self.P, self.qs, self.qf
        bm = self.qind[:, 2:3]
        P.emit("dve", lambda e: e.reduce_max(
            qs[0][:, 0:nb].unsqueeze(2), seg(src_sb, src_c0),
            axis=AX.X, apply_absolute_value=True), waits=waits)
        P.emit("dve", lambda e: e.tensor_single_scalar(
            qs[1][:, 0:nb], qs[0][:, 0:nb], 0.0, ALU.is_equal))
        t1 = P.emit("dve", lambda e: e.tensor_tensor(
            qs[2][:, 0:nb], qs[0][:, 0:nb], qs[1][:, 0:nb], ALU.add))
        t2 = P.emit("act", lambda e: e.activation(
            qs[3][:, 0:nb], qs[2][:, 0:nb], AF.Ln), waits=[t1])
        t3 = P.emit("dve", lambda e: e.tensor_scalar(
            qs[4][:, 0:nb], qs[3][:, 0:nb], INVLN2, 1e-9 - 0.5,
            ALU.mult, ALU.add), waits=[t2])
        te = self.emit_round(qs[4][:, 0:nb], qs[1][:, 0:nb],
                             qs[5][:, 0:nb], waits=[t3])
        P.emit("dve", lambda e: e.tensor_scalar(
            qs[4][:, 0:nb], qs[1][:, 0:nb], float(m + 2 - 31),
            float(m + 2 + 31), ALU.max, ALU.min), waits=[te])
        P.emit("dve", lambda e: e.tensor_single_scalar(
            qs[5][:, 0:nb], qs[4][:, 0:nb], float(m + 2), ALU.subtract))
        t5 = P.emit("act", lambda e: e.activation(
            qs[2][:, 0:nb], qs[5][:, 0:nb], AF.Exp, scale=-LN2),
            waits=[P.mark("dve")])
        t6 = P.emit("act", lambda e: e.activation(
            qs[3][:, 0:nb], qs[5][:, 0:nb], AF.Exp, scale=LN2))
        P.emit("dve", lambda e: e.tensor_single_scalar(
            qs[6][:, 0:nb], qs[5][:, 0:nb], 0.0, ALU.is_le))
        P.emit("dve", lambda e: e.tensor_tensor(
            qs[7][:, 0:nb], qs[2][:, 0:nb], qs[3][:, 0:nb], ALU.subtract),
            waits=[t5, t6])
        P.emit("dve", lambda e: e.tensor_tensor(
            qs[7][:, 0:nb], qs[7][:, 0:nb], qs[6][:, 0:nb], ALU.mult))
        P.emit("dve", lambda e: e.tensor_tensor(
            qs[7][:, 0:nb], qs[7][:, 0:nb], qs[3][:, 0:nb], ALU.add))
        P.emit("dve", lambda e: e.tensor_single_scalar(
            qs[7][:, 0:nb], qs[7][:, 0:nb], bm, ALU.add))
        P.emit("dve", lambda e: e.tensor_single_scalar(
            qs[7][:, 0:nb], qs[7][:, 0:nb], bm, ALU.subtract))
        P.emit("dve", lambda e: e.reciprocal(
            qs[4][:, 0:nb], qs[7][:, 0:nb]))
        P.emit("dve", lambda e: e.tensor_tensor(
            qs[2][:, 0:nb], qs[7][:, 0:nb], qs[4][:, 0:nb], ALU.subtract))
        P.emit("dve", lambda e: e.tensor_tensor(
            qs[2][:, 0:nb], qs[2][:, 0:nb], qs[6][:, 0:nb], ALU.mult))
        P.emit("dve", lambda e: e.tensor_tensor(
            qs[2][:, 0:nb], qs[2][:, 0:nb], qs[4][:, 0:nb], ALU.add))
        P.emit("dve", lambda e: e.tensor_tensor(
            qs[3][:, 0:nb], qs[7][:, 0:nb], qs[4][:, 0:nb], ALU.add))
        tS = P.emit("dve", lambda e: e.tensor_tensor(
            qs[3][:, 0:nb], qs[3][:, 0:nb], qs[2][:, 0:nb], ALU.subtract))
        t7 = P.emit("dve", lambda e: e.tensor_tensor(
            seg(qf[0], 0), seg(src_sb, src_c0), bc(qs[2]), ALU.mult))
        t8 = P.emit("act", lambda e: e.activation(
            qf[1][:, 0:F], qf[0][:, 0:F], AF.Abs), waits=[t7])
        P.emit("dve", lambda e: e.tensor_single_scalar(
            qf[2][:, 0:F], qf[1][:, 0:F], 2.0 * 2 ** m, ALU.is_ge),
            waits=[t8])
        tg4 = P.emit("dve", lambda e: e.tensor_single_scalar(
            qf[3][:, 0:F], qf[1][:, 0:F], 4.0 * 2 ** m, ALU.is_ge))
        tl1 = P.emit("act", lambda e: e.activation(
            qf[3][:, 0:F], qf[3][:, 0:F], AF.Copy, scale=2.0, bias=1.0),
            waits=[tg4])
        tl2 = P.emit("dve", lambda e: e.tensor_tensor(
            qf[3][:, 0:F], qf[3][:, 0:F], qf[2][:, 0:F], ALU.add),
            waits=[tl1])
        tl3 = P.emit("act", lambda e: e.activation(
            qf[3][:, 0:F], qf[3][:, 0:F], AF.Copy, scale=bm), waits=[tl2])
        P.emit("dve", lambda e: e.tensor_tensor(
            qf[1][:, 0:F], qf[0][:, 0:F], qf[3][:, 0:F], ALU.add),
            waits=[tl3])
        P.emit("dve", lambda e: e.tensor_tensor(
            qf[1][:, 0:F], qf[1][:, 0:F], qf[3][:, 0:F], ALU.subtract))
        P.emit("dve", lambda e: e.tensor_scalar(
            qf[1][:, 0:F], qf[1][:, 0:F], -cmax, cmax, ALU.max, ALU.min))
        tv = P.emit("dve", lambda e: e.tensor_tensor(
            seg(dst_sb, dst_c0), seg(qf[1], 0), bc(qs[3]), ALU.mult),
            waits=[tS, dst_guard])
        return tv

    def emit_T(self, pool, src_sb, src_c0, dst_sb, dst_c0, ident, waits,
               evac="dve"):
        i, pbank, guard = pool.next()
        tt = self.P.emit("pe", lambda e: e.transpose(
            pbank[:, 0:128], src_sb[:, src_c0:src_c0 + 128], ident[:]),
            waits=[waits, guard])
        if evac == "dve":
            tv = self.P.emit("dve", lambda e: e.tensor_copy(
                dst_sb[:, dst_c0:dst_c0 + 128], pbank[:, 0:128]),
                waits=[tt])
        else:
            tv = self.P.emit("act", lambda e: e.copy(
                dst_sb[:, dst_c0:dst_c0 + 128], pbank[:, 0:128]),
                waits=[tt])
        pool.consumed(i, tv)
        return tv

    def emit_mm_group(self, pool, mk_fn, waits, evac_fn, evac_eng="dve",
                      evac_waits=()):
        i, pbank, guard = pool.next()
        tm = self.P.emit("pe", lambda e, f=mk_fn, pb=pbank: f(e, pb),
                    waits=[waits, guard])
        tv = self.P.emit(evac_eng, lambda e, f=evac_fn, pb=pbank: f(e, pb),
                    waits=[tm, evac_waits])
        pool.consumed(i, tv)
        return tv



def build_nc():
    nc = bass.Bass("TRN2", target_bir_lowering=False,
                   detect_race_conditions=False)

    x_own = nc.dram_tensor("x_own", [TOK, H], F32, kind="ExternalInput")
    x_par = nc.dram_tensor("x_par", [TOK, H], F32, kind="ExternalInput")
    w_qkv = nc.dram_tensor("w_qkv", [3 * H, H], F32, kind="ExternalInput")
    bq_pp = nc.dram_tensor("bq_pp", [128, 48], F32, kind="ExternalInput")
    bv_row = nc.dram_tensor("bv_row", [1, H], F32, kind="ExternalInput")
    w_out = nc.dram_tensor("w_out", [H, H], F32, kind="ExternalInput")
    bo_row = nc.dram_tensor("bo_row", [1, H], F32, kind="ExternalInput")
    w_gate = nc.dram_tensor("w_gate", [II, H], F32, kind="ExternalInput")
    w_up = nc.dram_tensor("w_up", [II, H], F32, kind="ExternalInput")
    w_down = nc.dram_tensor("w_down", [H, II], F32, kind="ExternalInput")
    idf = nc.dram_tensor("idf", [128, 128], F32, kind="ExternalInput")
    idb = nc.dram_tensor("idb", [128, 128], BF16, kind="ExternalInput")
    out = nc.dram_tensor("out", [TOK, H], F32, kind="ExternalOutput")

    qiT_d = nc.dram_tensor("qiT_d", [16 * 128, 4 * TOK], BF16)
    ffn_d = nc.dram_tensor("ffn_d", [TOK, H], F32)
    wqkvT_d = nc.dram_tensor("wqkvT_d", [48 * 128, H], F32)
    woutT_d = nc.dram_tensor("woutT_d", [H, H], F32)
    K_d = nc.dram_tensor("K_d", [4 * 128, NHC * 512], F32)
    V_d = nc.dram_tensor("V_d", [4 * 128, 4 * H], F32)
    Q_d = nc.dram_tensor("Q_d", [2 * 128, NHC * 512], F32)

    es = ExitStack()
    with es:
        P = Plan(nc, es)
        sbg = lambda name, shape, dt: es.enter_context(
            nc.sbuf_tensor(name, shape, dt))

        sid_f = sbg("sid_f", [128, 128], F32)
        sid_b = sbg("sid_b", [128, 128], BF16)
        bq_sb = sbg("bq_sb", [128, 48], F32)
        ones1 = sbg("ones1", [1, 128], F32)
        sm_s = sbg("sm_s", [128, 8], F32)
        BL = Builder(nc, P, es)
        scope_ln = BL.scope_ln
        scope_q = BL.scope_q
        emit_ln = BL.emit_ln
        emit_round = BL.emit_round
        emit_quant = BL.emit_quant
        emit_T = BL.emit_T
        emit_mm_group = BL.emit_mm_group

        with nc.Block() as blk:
            P.emit("sp", lambda e: e.dma_start(bq_sb[:], bq_pp[:, :]),
                   inc=16, sem="s_misc")
            P.emit("sp", lambda e: e.dma_start(sid_f[:], idf[:, :]),
                   inc=16, sem="s_misc")
            P.emit("sp", lambda e: e.dma_start(sid_b[:], idb[:, :]),
                   inc=16, sem="s_misc")
            P.emit("dve", lambda e: e.memset(ones1[:], 1.0))
            P.flush(blk)

        scope_x = ExitStack()
        qxT = scope_x.enter_context(
            nc.sbuf_tensor("qxT", [128, NHC * TOK], BF16))
        if PHASES >= 2:
          with nc.Block() as blk:
            with (nc.psum_tensor("p1a", [128, 1024], BF16) as p1a,
                  nc.psum_tensor("p1b", [128, 1024], BF16) as p1b,
                  nc.sbuf_tensor("xn1", [128, H], F32) as xn1,
                  nc.sbuf_tensor("qx1", [128, H], BF16) as qx1):
                tp = PsumPool([p1a, p1b])
                prev_T = None
                for ti in range(8):
                    J = P.join()
                    tl = emit_ln(x_own, ti * 128, xn1, 0, head_waits=J)
                    tq1 = emit_quant(xn1, 0, qx1, 0, 2048, 3, [tl],
                                     dst_guard=prev_T)
                    for hc in range(NHC):
                        emit_T(tp, qx1, hc * 128, qxT, hc * TOK + ti * 128,
                               sid_b, [tq1], evac="act")
                    prev_T = P.mark("pe")
                P.flush(blk)

        if PHASES >= 3:
          with nc.Block() as blk:
            with (nc.psum_tensor("p2a", [128, 512], F32) as p2a,
                  nc.psum_tensor("p2b", [128, 512], F32) as p2b,
                  nc.psum_tensor("p2c", [128, 512], F32) as p2c,
                  nc.psum_tensor("p2t", [128, 1024], BF16) as p2t,
                  nc.psum_tensor("p2u", [128, 1024], BF16) as p2u,
                  nc.sbuf_tensor("wld0", [128, H], F32) as wld0,
                  nc.sbuf_tensor("wld1", [128, H], F32) as wld1,
                  nc.sbuf_tensor("qwn0", [128, H], BF16) as qwn0,
                  nc.sbuf_tensor("qwn1", [128, H], BF16) as qwn1,
                  nc.sbuf_tensor("qwgT", [128, NHC * 512], BF16) as qwgT,
                  nc.sbuf_tensor("qwuT", [128, NHC * 512], BF16) as qwuT,
                  nc.sbuf_tensor("gate", [128, 8 * 512], F32) as gate,
                  nc.sbuf_tensor("upb", [128, 8 * 512], F32) as upb,
                  nc.sbuf_tensor("qib", [128, 2048], BF16) as qib,
                  nc.sbuf_tensor("qiTb", [128, 4 * TOK], BF16) as qiTb):
                wldb = [wld0, wld1]
                qwnb = [qwn0, qwn1]
                tp = PsumPool([p2t, p2u])
                mp = PsumPool([p2a, p2b, p2c])
                J0 = P.join()
                tq_done = {-2: J0, -1: J0}
                tT_done = {-2: None, -1: None}
                t_mm_prev = None
                t_mul_prev = None
                kg = 0
                for ig in range(16):
                    for (wmat, slabT) in ((w_gate, qwgT), (w_up, qwuT)):
                        for ch in range(4):
                            b = kg % 2
                            r0 = ig * 512 + ch * 128
                            tl = P.emit("sp", lambda e, wmat=wmat, r0=r0,
                                        b=b: e.dma_start(
                                            wldb[b][:, 0:H],
                                            wmat[r0:r0 + 128, :]),
                                        waits=[tq_done[kg - 2]], inc=16,
                                        sem="s_wld")
                            tb = emit_quant(wldb[b], 0, qwnb[b], 0, 2048, 1,
                                            [tl],
                                            dst_guard=tT_done[kg - 2])
                            tq_done[kg] = tb
                            for hc in range(NHC):
                                emit_T(tp, qwnb[b], hc * 128, slabT,
                                       hc * 512 + ch * 128, sid_b,
                                       [tb, t_mm_prev],
                                       evac="act" if hc % 2 else "dve")
                            tT_done[kg] = P.mark("pe")
                            kg += 1
                    t_slab = [P.mark("dve"), P.mark("act")]
                    for (slabT, dst) in ((qwgT, gate), (qwuT, upb)):
                        for tc in range(8):
                            def mk(e, pb, slabT=slabT, tc=tc):
                                for hc in range(NHC):
                                    last = e.matmul(
                                        pb[:, 0:512],
                                        qxT[:, hc * TOK + tc * 128:
                                            hc * TOK + tc * 128 + 128],
                                        slabT[:, hc * 512:(hc + 1) * 512],
                                        start=(hc == 0), stop=(hc == NHC - 1))
                                return last
                            def ev(e, pb, dst=dst, tc=tc):
                                return e.copy(
                                    dst[:, tc * 512:(tc + 1) * 512],
                                    pb[:, 0:512])
                            emit_mm_group(mp, mk, [t_slab], ev, "act",
                                          evac_waits=[t_mul_prev])
                    t_mm_prev = P.mark("pe")
                    t_gu = P.mark("act")
                    t_sil = []
                    for tc in range(8):
                        t_sil.append(P.emit(
                            "act", lambda e, tc=tc: e.activation(
                                gate[:, tc * 512:(tc + 1) * 512],
                                gate[:, tc * 512:(tc + 1) * 512], AF.Silu),
                            waits=[t_gu]))
                    t_mul = []
                    for tc in range(8):
                        t_mul.append(P.emit(
                            "dve", lambda e, tc=tc: e.tensor_tensor(
                                gate[:, tc * 512:(tc + 1) * 512],
                                gate[:, tc * 512:(tc + 1) * 512],
                                upb[:, tc * 512:(tc + 1) * 512], ALU.mult),
                            waits=[t_sil[tc]]))
                    t_store_prev = P.mark("t_qit")
                    for tcg in range(2):
                        t_q = emit_quant(gate, tcg * 2048, qib, 0, 2048, 3,
                                         [t_mul[tcg * 4:tcg * 4 + 4]],
                                         dst_guard=P.mark("pe"))
                        for tcl in range(4):
                            tc = tcg * 4 + tcl
                            for q4 in range(4):
                                emit_T(tp, qib, tcl * 512 + q4 * 128, qiTb,
                                       q4 * TOK + tc * 128, sid_b,
                                       [t_q, t_store_prev],
                                       evac="act" if q4 % 2 else "dve")
                    t_mul_prev = P.mark("dve")
                    t_qiT = [P.mark("dve"), P.mark("act")]
                    P.emit("pl", lambda e, ig=ig: e.dma_start(
                        qiT_d[ig * 128:(ig + 1) * 128, :],
                        qiTb[:, 0:4 * TOK]), waits=t_qiT, inc=16,
                        sem="t_qit")
                P.flush(blk)
        scope_x.close()

        if PHASES >= 4:
          with nc.Block() as blk:
            with (nc.sbuf_tensor("wdld", [128, II], F32) as wdld,
                  nc.sbuf_tensor("qwdn", [128, II], BF16) as qwdn,
                  nc.sbuf_tensor("qwdT", [128, 64 * 512], BF16) as qwdT,
                  nc.sbuf_tensor("qiS0", [128, 4 * TOK], BF16) as qiS0,
                  nc.sbuf_tensor("qiS1", [128, 4 * TOK], BF16) as qiS1,
                  nc.sbuf_tensor("fout", [128, 512], F32) as fout):
                qiSb = [qiS0, qiS1]
                for hg in range(4):
                    J = P.join()
                    with (nc.psum_tensor(f"p3t_{hg}", [128, 1024], BF16)
                          as p3t,
                          nc.psum_tensor(f"p3u_{hg}", [128, 1024], BF16)
                          as p3u):
                        tp = PsumPool([p3t, p3u])
                        t_wread = J
                        tT_prev = None
                        for ch in range(4):
                            r0 = hg * 512 + ch * 128
                            tl = P.emit("sp", lambda e, r0=r0: e.dma_start(
                                wdld[:, 0:II], w_down[r0:r0 + 128, :]),
                                waits=t_wread, inc=16, sem="s_wdld")
                            tprev = tl
                            for sc in range(4):
                                tprev = emit_quant(wdld, sc * 2048, qwdn,
                                                   sc * 2048, 2048, 1,
                                                   [tprev],
                                                   dst_guard=tT_prev)
                            t_wread = [tprev]
                            for ic in range(64):
                                emit_T(tp, qwdn, ic * 128, qwdT,
                                       ic * 512 + ch * 128, sid_b, [tprev],
                                       evac="act" if ic % 2 else "dve")
                            tT_prev = P.mark("pe")
                        P.flush(blk)
                    t_slab = [P.mark("dve"), P.mark("act")]
                    with ExitStack() as ps8:
                        pd = [ps8.enter_context(nc.psum_tensor(
                            f"pd{i}_{hg}", [128, 512], F32))
                            for i in range(8)]
                        t_mmsl = {-2: None, -1: None}
                        for sl in range(16):
                            bq = sl % 2
                            tq = P.emit("sp", lambda e, sl=sl, bq=bq:
                                        e.dma_start(
                                            qiSb[bq][:, 0:4 * TOK],
                                            qiT_d[sl * 128:(sl + 1) * 128,
                                                  :]),
                                        waits=[t_slab, t_mmsl[sl - 2]],
                                        inc=16, sem="s_qis")
                            for tc in range(8):
                                def mk(e, sl=sl, tc=tc, bq=bq):
                                    for i4 in range(4):
                                        ic = sl * 4 + i4
                                        last = e.matmul(
                                            pd[tc][:, 0:512],
                                            qiSb[bq][:, i4 * TOK + tc * 128:
                                                i4 * TOK + tc * 128 + 128],
                                            qwdT[:, ic * 512:(ic + 1) * 512],
                                            start=(ic == 0), stop=(ic == 63))
                                    return last
                                P.emit("pe", lambda e, f=mk: f(e), waits=[tq])
                            t_mmsl[sl] = P.mark("pe")
                        t_mm = P.mark("pe")
                        t_fread = None
                        for tc in range(8):
                            tev = P.emit("act", lambda e, tc=tc: e.copy(
                                fout[:, 0:512], pd[tc][:, 0:512]),
                                waits=[t_mm, t_fread])
                            t_fread = P.emit(
                                "pl", lambda e, tc=tc, hg=hg: e.dma_start(
                                    ffn_d[tc * 128:(tc + 1) * 128,
                                          hg * 512:(hg + 1) * 512],
                                    fout[:, 0:512]), waits=[tev], inc=16,
                                sem="t_ffn")
                        P.flush(blk)
        scope_q.close()

        if PHASES >= 5:
          with nc.Block() as blk:
            with (nc.psum_tensor("p4a", [128, 512], F32) as p4a,
                  nc.psum_tensor("p4b", [128, 512], F32) as p4b,
                  nc.sbuf_tensor("wrow", [128, H], F32) as wrow,
                  nc.sbuf_tensor("wcol", [128, NHC * 128], F32) as wcol):
                tp = PsumPool([p4a, p4b])
                t_rd = None
                t_st = None
                for rc in range(48):
                    tl = P.emit("sp", lambda e, rc=rc: e.dma_start(
                        wrow[:, 0:H], w_qkv[rc * 128:(rc + 1) * 128, :]),
                        waits=[t_rd], inc=16, sem="s_wrow")
                    for hc in range(NHC):
                        emit_T(tp, wrow, hc * 128, wcol, hc * 128,
                               sid_f, [tl, t_st],
                               evac="act" if hc % 2 else "dve")
                    t_rd = P.mark("pe")
                    t_c = [P.mark("dve"), P.mark("act")]
                    t_st = P.emit("pl", lambda e, rc=rc: e.dma_start(
                        wqkvT_d[rc * 128:(rc + 1) * 128, :],
                        wcol[:, 0:H]), waits=[t_c], inc=16, sem="t_wT")
                for rc in range(16):
                    tl = P.emit("sp", lambda e, rc=rc: e.dma_start(
                        wrow[:, 0:H], w_out[rc * 128:(rc + 1) * 128, :]),
                        waits=[t_rd], inc=16, sem="s_wrow")
                    for hc in range(NHC):
                        emit_T(tp, wrow, hc * 128, wcol, hc * 128,
                               sid_f, [tl, t_st],
                               evac="act" if hc % 2 else "dve")
                    t_rd = P.mark("pe")
                    t_c = [P.mark("dve"), P.mark("act")]
                    for hc in range(NHC):
                        t_st = P.emit(
                            "pl", lambda e, rc=rc, hc=hc: e.dma_start(
                                woutT_d[hc * 128:(hc + 1) * 128,
                                        rc * 128:(rc + 1) * 128],
                                wcol[:, hc * 128:(hc + 1) * 128]),
                            waits=[t_c], inc=16, sem="t_wT")
                P.flush(blk)

        if PHASES >= 6:
          with nc.Block() as blk:
            with (nc.psum_tensor("p5a", [128, 512], F32) as p5a,
                  nc.psum_tensor("p5b", [128, 512], F32) as p5b,
                  nc.psum_tensor("p5c", [128, 512], F32) as p5c,
                  nc.psum_tensor("p5t", [128, 512], F32) as p5t,
                  nc.psum_tensor("p5u", [128, 512], F32) as p5u,
                  nc.sbuf_tensor("xnT", [128, NHC * 512], F32) as xnT,
                  nc.sbuf_tensor("xn5", [128, H], F32) as xn5,
                  nc.sbuf_tensor("wcolK", [128, H], F32) as wcolK,
                  nc.sbuf_tensor("wcolK1", [128, H], F32) as wcolK1,
                  nc.sbuf_tensor("kv_sb", [128, NHC * 512], F32) as kv_sb,
                  nc.sbuf_tensor("bvt", [128, 512], F32) as bvt):
                tp = PsumPool([p5t, p5u])
                mp = PsumPool([p5a, p5b, p5c])

                for kt in range(4):
                    J = P.join()
                    src = x_own if kt < 2 else x_par
                    t_prev = [J]
                    for st4 in range(4):
                        row0 = (kt * 512 + st4 * 128) % TOK
                        tl = emit_ln(src, row0, xn5, 0, head_waits=t_prev)
                        for hc in range(NHC):
                            emit_T(tp, xn5, hc * 128, xnT,
                                   hc * 512 + st4 * 128, sid_f, [tl])
                        t_prev = [P.mark("pe"), P.mark("dve")]
                    t_xnT = P.mark("dve")

                    wcb = [wcolK, wcolK1]
                    wk_hist = {-2: [J], -1: [J]}
                    t_kst = []
                    for oc in range(16):
                        bw = oc % 2
                        tw = P.emit("sp", lambda e, oc=oc, bw=bw: e.dma_start(
                            wcb[bw][:, 0:H],
                            wqkvT_d[(16 + oc) * 128:(17 + oc) * 128, :]),
                            waits=wk_hist[oc - 2], inc=16, sem="s_wk")
                        def mkK(e, pb, bw=bw):
                            for hc in range(NHC):
                                last = e.matmul(
                                    pb[:, 0:512],
                                    wcb[bw][:, hc * 128:(hc + 1) * 128],
                                    xnT[:, hc * 512:(hc + 1) * 512],
                                    start=(hc == 0), stop=(hc == NHC - 1))
                            return last
                        def evK(e, pb, oc=oc):
                            return e.tensor_single_scalar(
                                kv_sb[:, oc * 512:(oc + 1) * 512],
                                pb[:, 0:512], bq_sb[:, 16 + oc:17 + oc],
                                ALU.add)
                        emit_mm_group(mp, mkK, [tw, t_xnT], evK, "dve")
                        wk_hist[oc] = [P.mark("pe")]
                    t_k = P.mark("dve")
                    t_kst = [P.emit("pl", lambda e, kt=kt: e.dma_start(
                        K_d[kt * 128:(kt + 1) * 128, :],
                        kv_sb[:, 0:NHC * 512]), waits=[t_k], inc=16,
                        sem="t_K")]

                    if kt < 2:
                        wk_hist = {0: [wk_hist[14], t_kst],
                                   1: [wk_hist[15], t_kst]}
                        for oc in range(16):
                            bw = oc % 2
                            tw = P.emit("sp", lambda e, oc=oc, bw=bw:
                                        e.dma_start(
                                            wcb[bw][:, 0:H],
                                            wqkvT_d[oc * 128:
                                                    (oc + 1) * 128, :]),
                                        waits=wk_hist[oc - 2 if oc >= 2
                                                      else oc],
                                        inc=16, sem="s_wk")
                            def mkQ(e, pb, bw=bw):
                                for hc in range(NHC):
                                    last = e.matmul(
                                        pb[:, 0:512],
                                        wcb[bw][:, hc * 128:(hc + 1) * 128],
                                        xnT[:, hc * 512:(hc + 1) * 512],
                                        start=(hc == 0),
                                        stop=(hc == NHC - 1))
                                return last
                            def evQ(e, pb, oc=oc):
                                return e.tensor_scalar(
                                    kv_sb[:, oc * 512:(oc + 1) * 512],
                                    pb[:, 0:512], bq_sb[:, oc:oc + 1],
                                    RSQH, ALU.add, ALU.mult)
                            emit_mm_group(mp, mkQ, [tw], evQ, "dve")
                            wk_hist[oc] = [P.mark("pe")]
                        t_q = P.mark("dve")
                        t_kst = [P.emit("pl", lambda e, kt=kt: e.dma_start(
                            Q_d[kt * 128:(kt + 1) * 128, :],
                            kv_sb[:, 0:NHC * 512]), waits=[t_q], inc=16,
                            sem="t_Q")]

                    wk_hist = {0: [wk_hist[14], t_kst],
                               1: [wk_hist[15], t_kst]}
                    t_stage = None
                    for rc in range(16):
                        bw = rc % 2
                        tw = P.emit("sp", lambda e, rc=rc, bw=bw:
                                    e.dma_start(
                                        wcb[bw][:, 0:H],
                                        wqkvT_d[(32 + rc) * 128:
                                                (33 + rc) * 128, :]),
                                    waits=wk_hist[rc - 2 if rc >= 2 else rc],
                                    inc=16, sem="s_wk")
                        def mkV(e, pb, bw=bw):
                            for hc in range(NHC):
                                last = e.matmul(
                                    pb[:, 0:512],
                                    wcb[bw][:, hc * 128:(hc + 1) * 128],
                                    xnT[:, hc * 512:(hc + 1) * 512],
                                    start=(hc == 0), stop=(hc == NHC - 1))
                            return last
                        i, pb_, guard = mp.next()
                        tm = P.emit("pe", lambda e, f=mkV, pb=pb_: f(e, pb),
                                    waits=[tw, t_xnT, guard])
                        tvv = P.emit("dve", lambda e, pb=pb_, rc=rc:
                                     e.tensor_single_scalar(
                                         bvt[:, 0:512], pb[:, 0:512],
                                         bq_sb[:, 32 + rc:33 + rc], ALU.add),
                                     waits=[tm, t_stage])
                        mp.consumed(i, tvv)
                        for kc in range(4):
                            emit_T(tp, bvt, kc * 128, kv_sb,
                                   kc * 2048 + rc * 128, sid_f,
                                   [tvv, t_kst])
                        t_stage = P.mark("pe")
                        wk_hist[rc] = [P.mark("pe")]
                    t_v = P.mark("dve")
                    t_kst = [P.emit("pl", lambda e, kt=kt: e.dma_start(
                        V_d[kt * 128:(kt + 1) * 128, :],
                        kv_sb[:, 0:4 * H]), waits=[t_v], inc=16,
                        sem="t_V")]
                    P.emit("dve", None, waits=[t_kst], inc=0)
                P.flush(blk)
        scope_ln.close()

        if PHASES >= 7:
          with nc.Block() as blk:
            with ExitStack() as es6:
                ec6 = es6.enter_context
                probsT = ec6(nc.sbuf_tensor("probsT", [128, 16 * 512], F32))
                ctxT = ec6(nc.sbuf_tensor("ctxT", [128, 16 * 512], F32))
                ctx = ec6(nc.sbuf_tensor("ctx", [128, 4 * H], F32))
                bot = ec6(nc.sbuf_tensor("bot", [128, H], F32))
                rs4 = ec6(nc.sbuf_tensor("rs4", [128, 4], F32))
                pbuf = ec6(nc.sbuf_tensor("pbuf", [128, S], F32))
                scs2 = ec6(nc.sbuf_tensor("scs2", [128, 2 * S], F32))
                KVs = ec6(nc.sbuf_tensor("KVs", [128, 16 * 512], F32))
                p6a = ec6(nc.psum_tensor("p6a", [128, 512], F32))
                p6b = ec6(nc.psum_tensor("p6b", [128, 512], F32))
                p6t = ec6(nc.psum_tensor("p6t", [128, 512], F32))
                p6u = ec6(nc.psum_tensor("p6u", [128, 512], F32))
                mp = PsumPool([p6a, p6b])
                tp = PsumPool([p6t, p6u])
                tbo = P.emit("sp", lambda e: e.dma_start(
                    pbuf[0:1, 0:H], bo_row[:, :]), inc=16, sem="s_misc")
                for og in range(4):
                    def mkb(e, pb, og=og):
                        return e.matmul(pb[:, 0:512], ones1[:],
                                        pbuf[0:1, og * 512:(og + 1) * 512],
                                        start=True, stop=True)
                    def evb(e, pb, og=og):
                        return e.tensor_copy(
                            bot[:, og * 512:(og + 1) * 512], pb[:, 0:512])
                    emit_mm_group(mp, mkb, [tbo], evb, "dve")
                t_bo = P.mark("dve")

                for qh in range(2):
                    J = P.join()
                    with nc.sbuf_tensor(f"Qh_{qh}", [128, NHC * 512],
                                        F32) as Qh:
                        tlq = P.emit("sp", lambda e, qh=qh: e.dma_start(
                            Qh[:, 0:NHC * 512],
                            Q_d[qh * 128:(qh + 1) * 128, :]),
                            waits=[J], inc=16, sem="s_qh")
                        t_kread = [J]
                        t2h = {-1: None}
                        for qcg in range(2):
                            for kt in range(4):
                                tlk = P.emit("sp", lambda e, kt=kt:
                                             e.dma_start(
                                                 KVs[:, 0:NHC * 512],
                                                 K_d[kt * 128:
                                                     (kt + 1) * 128, :]),
                                             waits=t_kread, inc=16,
                                             sem="s_kvs")
                                for qc2 in range(2):
                                    qc = qcg * 2 + qc2
                                    def mkS(e, pb, qc=qc):
                                        for oc in range(16):
                                            last = e.matmul(
                                                pb[:, 0:512],
                                                Qh[:, oc * 512 + qc * 128:
                                                   oc * 512 + qc * 128
                                                   + 128],
                                                KVs[:, oc * 512:
                                                    (oc + 1) * 512],
                                                start=(oc == 0),
                                                stop=(oc == 15))
                                        return last
                                    def evS(e, pb, kt=kt, qc2=qc2):
                                        return e.tensor_copy(
                                            scs2[:, qc2 * S + kt * 512:
                                                 qc2 * S + kt * 512 + 512],
                                            pb[:, 0:512])
                                    emit_mm_group(mp, mkS, [tlq, tlk], evS,
                                                  "dve",
                                                  evac_waits=[t2h[qcg - 1]])
                                t_kread = [P.mark("pe")]
                            tg = []
                            for qc2 in range(2):
                                qc = qcg * 2 + qc2
                                t_sc = P.mark("dve")
                                t1 = P.emit("dve", lambda e, qc=qc, qc2=qc2:
                                            e.reduce_max(
                                                sm_s[:, qc:qc + 1],
                                                scs2[:, qc2 * S:
                                                     (qc2 + 1) * S],
                                                axis=AX.X, negate=True),
                                            waits=[t_sc])
                                t2 = P.emit("act", lambda e, qc=qc, qc2=qc2:
                                            e.activation(
                                                pbuf[:, 0:S],
                                                scs2[:, qc2 * S:
                                                     (qc2 + 1) * S],
                                                AF.Exp,
                                                bias=sm_s[:, qc:qc + 1],
                                                accum_out=sm_s[:, 4 + qc:
                                                               5 + qc]),
                                            waits=[t1, P.mark("pe")])
                                tg.append(t2)
                                P.emit("dve", lambda e, qc=qc: e.reciprocal(
                                    rs4[:, qc:qc + 1],
                                    sm_s[:, 4 + qc:5 + qc]), waits=[t2])
                                for kc in range(16):
                                    emit_T(tp, pbuf, kc * 128, probsT,
                                           kc * 512 + qc * 128, sid_f, [t2])
                            t2h[qcg] = tg
                            t_kread = [P.mark("pe")]
                        t_probs = P.mark("dve")
                        tz = []
                        for qc in range(4):
                            tz.append(P.emit("dve", lambda e, qc=qc:
                                             e.memset(
                                                 ctx[:, qc * H:(qc + 1) * H],
                                                 0.0), waits=[J]))
                        t_vread = [P.mark("pe")]
                        for kt in range(4):
                            tlv = P.emit("sp", lambda e, kt=kt: e.dma_start(
                                KVs[:, 0:4 * H],
                                V_d[kt * 128:(kt + 1) * 128, :]),
                                waits=t_vread, inc=16, sem="s_kvs")
                            for qc in range(4):
                                for vg in range(4):
                                    def mkC(e, pb, kt=kt, qc=qc, vg=vg):
                                        for kc in range(4):
                                            kcg = kt * 4 + kc
                                            last = e.matmul(
                                                pb[:, 0:512],
                                                probsT[:,
                                                       kcg * 512 + qc * 128:
                                                       kcg * 512 + qc * 128
                                                       + 128],
                                                KVs[:, kc * 2048 + vg * 512:
                                                    kc * 2048 +
                                                    (vg + 1) * 512],
                                                start=(kc == 0),
                                                stop=(kc == 3))
                                        return last
                                    def evC(e, pb, qc=qc, vg=vg):
                                        return e.scalar_tensor_tensor(
                                            ctx[:, qc * H + vg * 512:
                                                qc * H + (vg + 1) * 512],
                                            pb[:, 0:512], rs4[:, qc:qc + 1],
                                            ctx[:, qc * H + vg * 512:
                                                qc * H + (vg + 1) * 512],
                                            ALU.mult, ALU.add)
                                    emit_mm_group(mp, mkC,
                                                  [tlv, t_probs, tz[qc]],
                                                  evC, "dve")
                            t_vread = [P.mark("pe")]
                        t_ctx = P.mark("dve")
                        for qc in range(4):
                            for vc in range(16):
                                emit_T(tp, ctx, qc * H + vc * 128, ctxT,
                                       vc * 512 + qc * 128, sid_f, [t_ctx])
                        t_ctxT = P.mark("dve")
                        P.flush(blk)
                    with ExitStack() as ps4:
                        wot = ps4.enter_context(nc.sbuf_tensor(
                            f"wot_{qh}", [128, 2 * 512], F32))
                        xf = ps4.enter_context(nc.sbuf_tensor(
                            f"xf_{qh}", [128, 512], F32))
                        ff = ps4.enter_context(nc.sbuf_tensor(
                            f"ff_{qh}", [128, 512], F32))
                        oo = ps4.enter_context(nc.sbuf_tensor(
                            f"oo_{qh}", [128, 512], F32))
                        po = [ps4.enter_context(nc.psum_tensor(
                            f"po{i}_{qh}", [128, 512], F32))
                            for i in range(4)]
                        t_oo = None
                        tws = [None, None]
                        for og in range(4):
                            t_ev_prev = P.mark("dve")
                            for vc in range(16):
                                par = vc % 2
                                tw = P.emit(
                                    "sp",
                                    lambda e, vc=vc, og=og, par=par:
                                    e.dma_start(
                                        wot[:, par * 512:(par + 1) * 512],
                                        woutT_d[vc * 128:(vc + 1) * 128,
                                                og * 512:(og + 1) * 512]),
                                    waits=[tws[par], t_ctxT], inc=16,
                                    sem=f"s_wot{par}")
                                def mko(e, vc=vc, par=par):
                                    for qc in range(4):
                                        last = e.matmul(
                                            po[qc][:, 0:512],
                                            ctxT[:, vc * 512 + qc * 128:
                                                 vc * 512 + qc * 128 + 128],
                                            wot[:, par * 512:
                                                (par + 1) * 512],
                                            start=(vc == 0), stop=(vc == 15))
                                    return last
                                tm = P.emit("pe", lambda e, f=mko: f(e),
                                            waits=[tw, t_ctxT, t_ev_prev])
                                tws[par] = tm
                            t_mm = P.mark("pe")
                            for qc in range(4):
                                row0 = qh * 512 + qc * 128
                                tx = P.emit("sp",
                                            lambda e, row0=row0, og=og:
                                            e.dma_start(
                                                xf[:, 0:512],
                                                x_own[row0:row0 + 128,
                                                      og * 512:
                                                      (og + 1) * 512]),
                                            waits=[t_oo], inc=16,
                                            sem="s_xf")
                                tf = P.emit("sp",
                                            lambda e, row0=row0, og=og:
                                            e.dma_start(
                                                ff[:, 0:512],
                                                ffn_d[row0:row0 + 128,
                                                      og * 512:
                                                      (og + 1) * 512]),
                                            waits=[t_oo], inc=16,
                                            sem="s_ff")
                                P.emit("dve", lambda e, qc=qc:
                                       e.tensor_tensor(
                                           oo[:, 0:512], po[qc][:, 0:512],
                                           xf[:, 0:512], ALU.add),
                                       waits=[t_mm, tx, t_oo])
                                P.emit("dve", lambda e: e.tensor_tensor(
                                    oo[:, 0:512], oo[:, 0:512],
                                    ff[:, 0:512], ALU.add), waits=[tf])
                                t3 = P.emit("dve", lambda e, og=og:
                                            e.tensor_tensor(
                                                oo[:, 0:512], oo[:, 0:512],
                                                bot[:, og * 512:
                                                    (og + 1) * 512],
                                                ALU.add), waits=[t_bo])
                                t_oo = P.emit("pl",
                                              lambda e, row0=row0, og=og:
                                              e.dma_start(
                                                  out[row0:row0 + 128,
                                                      og * 512:
                                                      (og + 1) * 512],
                                                  oo[:, 0:512]),
                                              waits=[t3], inc=16,
                                              sem="t_out")
                        P.flush(blk)

        with nc.Block() as blk:
            P.emit("pl", None, waits=P.join(), inc=0, allow_self=True)
            P.flush(blk)

    return nc


_NC_CACHE = None


def kernel(L0_attn_hidden_states, L0_attn_w_qkv, L0_attn_b_qkv, L0_attn_w_out,
           L0_attn_b_out, L0_ffn_w_gate, L0_ffn_w_up, L0_ffn_w_down):
    global _NC_CACHE
    x = np.ascontiguousarray(np.asarray(L0_attn_hidden_states,
                                        dtype=np.float32))
    w_qkv_a = np.ascontiguousarray(np.asarray(L0_attn_w_qkv,
                                              dtype=np.float32))
    b_qkv = np.asarray(L0_attn_b_qkv, dtype=np.float32)
    w_out_a = np.ascontiguousarray(np.asarray(L0_attn_w_out,
                                              dtype=np.float32))
    b_out = np.asarray(L0_attn_b_out, dtype=np.float32)
    w_gate_a = np.ascontiguousarray(np.asarray(L0_ffn_w_gate,
                                               dtype=np.float32))
    w_up_a = np.ascontiguousarray(np.asarray(L0_ffn_w_up, dtype=np.float32))
    w_down_a = np.ascontiguousarray(np.asarray(L0_ffn_w_down,
                                               dtype=np.float32))

    bq_pp = np.ascontiguousarray(b_qkv.reshape(48, 128).T)
    bv_row = np.ascontiguousarray(b_qkv[2 * H:3 * H].reshape(1, H))
    bo_row = np.ascontiguousarray(b_out.reshape(1, H))
    idf = np.eye(128, dtype=np.float32)
    idb = np.eye(128, dtype=ml_dtypes.bfloat16)

    if _NC_CACHE is None:
        _NC_CACHE = build_nc()
    nc = _NC_CACHE

    in_maps = []
    for c in range(8):
        b, h = c // 2, c % 2
        xo = np.ascontiguousarray(x[b, h * 1024:(h + 1) * 1024, :])
        xp = np.ascontiguousarray(x[b, (1 - h) * 1024:(2 - h) * 1024, :])
        in_maps.append({
            "x_own": xo, "x_par": xp,
            "w_qkv": w_qkv_a, "bq_pp": bq_pp, "bv_row": bv_row,
            "w_out": w_out_a, "bo_row": bo_row,
            "w_gate": w_gate_a, "w_up": w_up_a, "w_down": w_down_a,
            "idf": idf, "idb": idb,
        })

    res = run_bass_kernel_spmd(nc, in_maps, core_ids=list(range(8)))
    outp = np.empty((B, S, H), dtype=np.float32)
    for c in range(8):
        b, h = c // 2, c % 2
        outp[b, h * 1024:(h + 1) * 1024, :] = res.results[c]["out"]
    return outp
